# version 30
# baseline (speedup 1.0000x reference)
import math
from contextlib import ExitStack

import numpy as np
import ml_dtypes

import concourse.bass as bass
import concourse.mybir as mybir
from concourse.bass_utils import run_bass_kernel_spmd

F32 = mybir.dt.float32
BF16 = mybir.dt.bfloat16
I32 = mybir.dt.int32
AF = mybir.ActivationFunctionType
ALU = mybir.AluOpType
AX = mybir.AxisListType

T = 8192
C = 2048
D = 128
FF = 8192
NC = 8
TOWN = 1024
NG = 16
GQ = 512
ISPAN = 1024
NPASS = FF // ISPAN
SCALE = 1.0 / math.sqrt(D)
EPS = 1e-5

PRODUCERS = {
    "tT": "tensor", "sS": "scalar", "vV": "vector",
    "qS": "", "qC": "", "qG": "", "qV": "", "cc": "",
}


class Walk:

    def __init__(self, nc, which, eng, sems):
        self.nc = nc
        self.which = which
        self.eng = eng
        self.sems = sems
        self.cnt = {k: 0 for k in sems}
        self.last_wait = {k: 0 for k in sems}

    def _wait(self, deps):
        for dep in deps:
            if dep is None:
                continue
            if isinstance(dep, list):
                self._wait(dep)
                continue
            s, v = dep
            if PRODUCERS[s] == self.which:
                continue
            if self.last_wait[s] >= v:
                continue
            self.eng.wait_ge(self.sems[s], v)
            self.last_wait[s] = v

    def do(self, engine, fn, deps=(), inc=None, n=1):
        tok = None
        if inc is not None:
            self.cnt[inc] += n
            tok = (inc, self.cnt[inc])
        if self.which == engine:
            self._wait(deps)
            ins = fn(self.eng)
            if inc is not None:
                ins.then_inc(self.sems[inc], n)
        return tok

    def wait_only(self, engine, deps):
        if self.which == engine:
            self._wait(deps)


def build_program():
    nc = bass.Bass("TRN2", target_bir_lowering=False, debug=False,
                   detect_race_conditions=False)

    H = {}
    H["nc"] = nc
    def inp(name, shape, dtype):
        H[name] = nc.dram_tensor(name, shape, dtype, kind="ExternalInput").ap()
    inp("xfeat", [C, T], F32)
    inp("xown", [C, TOWN], F32)
    inp("watt", [C, 768], F32)
    inp("bcol", [128, 4], F32)
    inp("bvrow", [1, 256], F32)
    inp("wpj", [256, C], F32)
    inp("bproj", [128, 16], F32)
    inp("gateT", [C, FF], I32)
    inp("upT", [C, FF], I32)
    inp("downT", [FF, C], I32)
    inp("gsc", [128, 64], F32)
    inp("usc", [128, 64], F32)
    inp("dsc", [128, 16], F32)
    inp("maskp", [128, 2048], BF16)
    inp("onescol", [128, 1], BF16)
    inp("onesrow", [1, 128], F32)
    H["out"] = nc.dram_tensor("out", [C, TOWN], F32, kind="ExternalOutput").ap()

    H["rs_in"] = [nc.dram_tensor(f"rs_in{j}", [NC, C, 128], F32, kind="Internal")
                  for j in range(8)]
    H["rs_out"] = [nc.dram_tensor(f"rs_out{j}", [C, 128], F32, kind="Internal")
                   for j in range(8)]

    with ExitStack() as ctx:
        e = ctx.enter_context
        base = (nc.sbuf_base + 31) & ~31
        assert nc.sbuf_top - base >= 206 * 1024, (nc.sbuf_base, nc.sbuf_top)

        def at(name, shape, dtype, off_kb):
            t = nc.alloc_sbuf_tensor_at(name, shape, dtype,
                                        offset=base + int(off_kb * 1024))
            H[name] = t
            return t

        at("ones_c", [128, 1], BF16, 0.0)
        at("ones_r", [1, 128], F32, 0.03125)
        at("bcol_sb", [128, 4], F32, 0.59375)
        at("bvrow_f", [1, 256], F32, 0.625)
        at("bvrow_b", [1, 256], BF16, 1.625)
        at("gsc_sb", [128, 64], F32, 2.125)
        at("usc_sb", [128, 64], F32, 2.375)
        at("dsc_sb", [128, 16], F32, 2.625)
        at("bproj_sb", [128, 16], F32, 2.6875)
        at("negm_sb", [128, 2], F32, 2.75)
        at("ones_rb", [1, 128], BF16, 2.78125)
        at("rowA", [1, 1024], F32, 4.0)
        at("rowB", [1, 1024], F32, 8.0)
        at("rbb16", [128, 512], BF16, 10.0)
        at("mrb16", [128, 512], BF16, 14.0)
        at("rowD", [1, 1024], F32, 12.0)
        at("rowS", [1, 1024], F32, 16.0)
        at("rowW", [1, 768], F32, 20.0)

        OV = 23.0
        at("Q_sb", [128, 2 * T], BF16, OV)
        at("K_sb", [128, 2 * T], BF16, OV + 32)
        at("V_sb", [128, 64 * 256], BF16, OV + 64)
        H["xc_sb"] = [at(f"xc{i}", [128, 16 * GQ], BF16, OV + 96 + 16 * i)
                      for i in range(2)]
        H["xq_sb"] = [at(f"xq{i}", [128, 16 * GQ], BF16, OV + 128 + 16 * i)
                      for i in range(2)]
        at("watt_sb", [128, 16 * 768], BF16, OV + 160)
        at("o_sb", [128, 2 * T], BF16, OV + 96)
        at("prst_sb", [128, 16 * GQ], F32, OV + 128)
        H["p_sb"] = [at(f"p{i}", [128, 1024], BF16, OV + 160 + 2 * i)
                     for i in range(2)]
        at("wp_sb", [128, 2 * C], BF16, OV + 164)
        at("maskp_sb", [128, 2048], BF16, OV + 172)
        at("rbs_sb", [128, 512], F32, OV + 176)
        H["sqs_sb"] = [at(f"sqs{i}", [128, 512], BF16, OV + 178 + i)
                       for i in range(2)]
        at("rowG", [1, 512], BF16, 4.0)
        at("rowR2", [1, 512], BF16, 5.0)
        at("rowLn", [1, 512], F32, 6.0)
        at("x2_sb", [128, 16 * 1024], F32, OV)
        at("ln2_sb", [128, 16 * 1024], BF16, OV + 64)
        H["wbuf"] = [at(f"wbuf{i}", [128, 16384], BF16, OV + 96 + 32 * i)
                     for i in range(2)]
        at("hact_sb", [128, 8 * 1024], BF16, OV + 160)
        H["gtmp_sb"] = [at(f"gtmp{i}", [128, 512], BF16, OV + 176 + i)
                        for i in range(2)]
        at("r2bb16", [128, 1024], BF16, OV + 160)
        at("mr2bb16", [128, 1024], BF16, OV + 162)
        H["rsld2"] = [at("rsld_a", [128, 16 * 128], F32, OV + 172),
                      at("rsld_b", [128, 16 * 128], F32, OV + 152)]

        H["pp"] = [e(nc.psum_tensor(f"pp{i}", [128, 1024], F32))
                   for i in range(4)]

        sems = {name: e(nc.semaphore(name)) for name in PRODUCERS}
        block = e(nc.Block())

        def run(which):
            def body(eng):
                emit_all(Walk(nc, which, eng, sems), H)
            return body

        block.sync(run("sync"))
        block.tensor(run("tensor"))
        block.scalar(run("scalar"))
        block.vector(run("vector"))
        block.gpsimd(run("gpsimd"))

    return nc



def emit_all(W, H):
    pp = H["pp"]
    ones_c, ones_r = H["ones_c"], H["ones_r"]
    rowA, rowB = H["rowA"], H["rowB"]
    rowD, rowS, rowW, rowG = H["rowD"], H["rowS"], H["rowW"], H["rowG"]
    Q_sb, K_sb, V_sb, watt_sb = H["Q_sb"], H["K_sb"], H["V_sb"], H["watt_sb"]
    xc_sb, xq_sb = H["xc_sb"], H["xq_sb"]

    toks = []
    for dname, sname in (("bcol_sb", "bcol"), ("bvrow_f", "bvrow"),
                         ("gsc_sb", "gsc"), ("usc_sb", "usc"),
                         ("dsc_sb", "dsc"), ("bproj_sb", "bproj"),
                         ("ones_r", "onesrow"), ("ones_c", "onescol")):
        dst, src = H[dname], H[sname]
        full = tuple(slice(None) for _ in dst.shape)
        toks.append(W.do("sync", lambda e, d=dst, s=src, f=full: e.dma_start(
            out=d[f], in_=s), inc="qS", n=16))
    tok_const = toks[-1]
    tok_bvb = W.do("vector", lambda e: e.tensor_copy(
        H["bvrow_b"][:, :], H["bvrow_f"][:, :]), deps=[toks[-1]], inc="vV")
    tok_orb = W.do("gpsimd", lambda e: e.dma_start(
        out=H["ones_rb"][:, :], in_=H["onesrow"]), inc="qG", n=16)
    tok_watt = W.do("gpsimd", lambda e: e.dma_start(
        out=watt_sb[:, :].rearrange("p (c n) -> p c n", n=768),
        in_=H["watt"].rearrange("(c p) n -> p c n", p=128)), inc="qG", n=16)

    sx_ps = pp[0][0:1, 0:512]
    sq_ps = pp[0][0:1, 512:1024]
    rb_ps = pp[1][:, 0:512]
    mrb_ps = pp[1][:, 512:1024]
    qkv_ps = [pp[2][:, 0:512], pp[2][:, 512:1024]]
    v_ps = [pp[3][:, 0:256], pp[3][:, 512:768]]

    war_xc_h = [[[], []], [[], []]]
    war_xq = [[], []]
    war_sx = [tok_watt]
    war_rb = []
    war_rbb = []
    war_mrb = []
    war_mrbb = []
    war_qkv = [[], [], [], []]
    war_vps = [[], [], [], []]
    war_rowA = []
    war_rowB = []
    war_rowD = []
    last_p1_evac = None
    last_p1_mm = None

    def p1_prep(g):
        b = g % 2
        xcg, xqg = xc_sb[b], xq_sb[b]
        tok_ld = W.do("gpsimd", lambda e, g=g, xcg=xcg: e.dma_start(
            out=xcg[:, :].rearrange("p (c t) -> p c t", t=GQ),
            in_=H["xfeat"][:, g * GQ:(g + 1) * GQ].rearrange(
                "(c p) t -> p c t", p=128)),
            deps=[war_xc_h[b][0], war_xc_h[b][1]], inc="qG", n=16)
        tok_sx = None
        for hm in range(16):
            tok_sx = W.do("tensor", lambda e, hm=hm, xcg=xcg: e.matmul(
                sx_ps, ones_c[:, :], xcg[:, hm * GQ:(hm + 1) * GQ],
                start=hm == 0, stop=hm == 15),
                deps=[tok_ld, war_sx, tok_const] if hm == 0 else (),
                inc="tT" if hm == 15 else None)
        sq_toks = []
        for sb_i in range(4):
            tok = W.do("scalar", lambda e, sb_i=sb_i, xcg=xcg, xqg=xqg:
                       e.activation(xqg[:, sb_i * 2048:(sb_i + 1) * 2048],
                                    xcg[:, sb_i * 2048:(sb_i + 1) * 2048],
                                    AF.Square),
                       deps=[tok_ld, war_xq[b]] if sb_i == 0 else (),
                       inc="sS")
            sq_toks.append(tok)
        tok_sq = None
        for hm in range(16):
            tok_sq = W.do("tensor", lambda e, hm=hm, xqg=xqg: e.matmul(
                sq_ps, ones_c[:, :], xqg[:, hm * GQ:(hm + 1) * GQ],
                start=hm == 0, stop=hm == 15),
                deps=[sq_toks[hm // 4]] if hm % 4 == 0 else (),
                inc="tT" if hm == 15 else None)
        W.do("scalar", lambda e: e.mul(rowA[:, 0:512], sx_ps, 1.0 / 2048.0),
             deps=[tok_sx, war_rowA])
        tok_st = W.do("scalar", lambda e: e.mul(
            rowB[:, 0:512], sq_ps, 1.0 / 2048.0),
            deps=[tok_sq, war_rowB], inc="sS")
        return tok_st, tok_sx, tok_sq

    def p1_statsvec(tok_st, tok_rb_prev):
        W.do("vector", lambda e: e.tensor_mul(
            rowS[:, 0:512], rowA[:, 0:512], rowA[:, 0:512]),
            deps=[tok_st, tok_rb_prev])
        W.do("vector", lambda e: e.scalar_tensor_tensor(
            rowB[:, 0:512], rowB[:, 0:512], EPS, rowS[:, 0:512],
            op0=ALU.add, op1=ALU.subtract))
        tok_vv = W.do("vector", lambda e: e.tensor_copy(
            rowB[:, 0:512], rowB[:, 0:512]), inc="vV")
        W.do("scalar", lambda e: e.activation(
            rowB[:, 0:512], rowB[:, 0:512], AF.Ln), deps=[tok_vv])
        tok_r = W.do("scalar", lambda e: e.activation(
            rowS[:, 0:512], rowB[:, 0:512], AF.Exp, scale=-0.5), inc="sS")
        tok_mr = W.do("vector", lambda e: e.tensor_mul(
            rowD[:, 0:512], rowA[:, 0:512], rowS[:, 0:512]),
            deps=[tok_r, war_rowD], inc="vV")
        return tok_r, tok_mr

    st0, sx0, sq0 = p1_prep(0)
    prep = {"st": st0, "sx": sx0, "sq": sq0}
    tok_r, tok_mr = p1_statsvec(st0, None)

    for g in range(NG):
        b = g % 2
        xcg, xqg = xc_sb[b], xq_sb[b]
        cur_st, cur_sx, cur_sq = prep["st"], prep["sx"], prep["sq"]
        cur_r, cur_mr = tok_r, tok_mr
        war_sx = [cur_st]
        war_rowA = [cur_mr]
        war_rowB = [cur_r]
        tok_rb = W.do("tensor", lambda e: e.matmul(
            rb_ps, ones_r[:, :], rowS[0:1, 0:512], start=True, stop=True),
            deps=[cur_r, war_rb, tok_const], inc="tT")
        tok_mb = W.do("tensor", lambda e: e.matmul(
            mrb_ps, ones_r[:, :], rowD[0:1, 0:512], start=True, stop=True),
            deps=[cur_mr, war_mrb], inc="tT")
        war_rowD = [tok_mb]
        tok_rbc = W.do("vector", lambda e: e.tensor_copy(
            H["rbb16"][:, :], rb_ps), deps=[tok_rb, war_rbb], inc="vV")
        tok_mbc = W.do("vector", lambda e: e.tensor_copy(
            H["mrb16"][:, :], mrb_ps), deps=[tok_mb, war_mrbb], inc="vV")
        tt_toks = []
        for hm in range(16):
            W.do("vector", lambda e, hm=hm, xcg=xcg, xqg=xqg: e.tensor_tensor(
                xqg[:, hm * GQ:(hm + 1) * GQ],
                xcg[:, hm * GQ:(hm + 1) * GQ],
                H["rbb16"][:, :], op=ALU.mult),
                deps=[cur_sq] if hm == 0 else ())
            tok = W.do("vector", lambda e, hm=hm, xqg=xqg: e.tensor_tensor(
                xqg[:, hm * GQ:(hm + 1) * GQ],
                xqg[:, hm * GQ:(hm + 1) * GQ],
                H["mrb16"][:, :], op=ALU.subtract), inc="vV")
            tt_toks.append(tok)
        if g + 1 < NG:
            nst, nsx, nsq = p1_prep(g + 1)
            prep = {"st": nst, "sx": nsx, "sq": nsq}
            tok_r, tok_mr = p1_statsvec(prep["st"], tok_rb)
        qk_slots = [pp[2][:, 0:512], pp[2][:, 512:1024],
                    rb_ps, mrb_ps]
        v_slots = [pp[3][:, 0:256], pp[3][:, 512:768]]
        qk_last = [None] * 4
        v_last = [None] * 4
        for hm in range(16):
            for cc in range(4):
                qk_last[cc] = W.do("tensor", lambda e, hm=hm, cc=cc,
                                   xqg=xqg: e.matmul(
                    qk_slots[cc], watt_sb[:, hm * 768 + cc * 128:
                                          hm * 768 + (cc + 1) * 128],
                    xqg[:, hm * GQ:(hm + 1) * GQ],
                    start=hm == 0, stop=hm == 15),
                    deps=[tt_toks[hm], war_qkv[cc],
                          tok_rbc if cc == 2 else None,
                          tok_mbc if cc == 3 else None],
                    inc="tT" if hm == 15 else None)
            for tt in range(2):
                v_last[tt] = W.do("tensor", lambda e, hm=hm, tt=tt,
                                  xqg=xqg: e.matmul(
                    v_slots[tt], xqg[:, hm * GQ + tt * 128:
                                     hm * GQ + (tt + 1) * 128],
                    watt_sb[:, hm * 768 + 512: hm * 768 + 768],
                    start=hm == 0, stop=hm == 15),
                    deps=[tt_toks[hm], war_vps[tt]],
                    inc="tT" if hm == 15 else None)
        for tt in range(2):
            kt = 4 * g + tt
            ev = W.do("scalar", lambda e, kt=kt, tt=tt: e.activation(
                V_sb[:, kt * 256:(kt + 1) * 256], v_slots[tt], AF.Identity),
                deps=[v_last[tt]], inc="sS")
            war_vps[tt] = [ev]
        for hm in range(16):
            for tt in range(2, 4):
                v_last[tt] = W.do("tensor", lambda e, hm=hm, tt=tt,
                                  xqg=xqg: e.matmul(
                    v_slots[tt - 2], xqg[:, hm * GQ + tt * 128:
                                         hm * GQ + (tt + 1) * 128],
                    watt_sb[:, hm * 768 + 512: hm * 768 + 768],
                    start=hm == 0, stop=hm == 15),
                    deps=[war_vps[tt - 2], war_vps[tt]] if hm == 0 else (),
                    inc="tT" if hm == 15 else None)
        for cc in range(4):
            dst = Q_sb if cc < 2 else K_sb
            hh = cc % 2
            ev = W.do("scalar", lambda e, dst=dst, hh=hh, g=g, cc=cc:
                      e.activation(dst[:, hh * T + g * GQ:
                                       hh * T + (g + 1) * GQ],
                                   qk_slots[cc], AF.Identity,
                                   bias=H["bcol_sb"][:, cc:cc + 1]),
                      deps=[qk_last[cc]], inc="sS")
            war_qkv[cc] = [ev]
        war_rb = [war_qkv[2]]
        war_mrb = [war_qkv[3]]
        for tt in range(2, 4):
            kt = 4 * g + tt
            ev = W.do("scalar", lambda e, kt=kt, tt=tt: e.activation(
                V_sb[:, kt * 256:(kt + 1) * 256], v_slots[tt - 2],
                AF.Identity),
                deps=[v_last[tt]], inc="sS")
            war_vps[tt] = [ev]
            last_p1_evac = ev
        last_p1_mm = v_last[3]
        war_xq[b] = [v_last[3], tt_toks[15]]
        war_rbb = [tt_toks[15]]
        war_mrbb = [tt_toks[15]]
        war_xc_h[b][0] = [tt_toks[7], cur_sx]
        war_xc_h[b][1] = [tt_toks[15], cur_sx]

    ph1 = [last_p1_evac, last_p1_mm]

    sqs_sb = H["sqs_sb"]
    qn_ps = pp[1][0:1, 512:1024]
    negm_toks = []
    war_sqs = [[ph1], [ph1]]
    war_qn = []
    tok_bc = None
    for h in range(2):
        for si, src in enumerate((Q_sb, K_sb)):
            run_off = h * 2 + si
            for ck in range(16):
                sb = sqs_sb[ck % 2]
                tok_q2 = W.do("scalar", lambda e, src=src, h=h, ck=ck, sb=sb:
                              e.activation(sb[:, :],
                                           src[:, h * T + ck * 512:
                                               h * T + (ck + 1) * 512],
                                           AF.Square),
                              deps=[ph1, war_sqs[ck % 2]], inc="sS")
                tok_nm = W.do("tensor", lambda e, sb=sb: e.matmul(
                    qn_ps, ones_c[:, :], sb[:, :], start=True, stop=True),
                    deps=[tok_q2, war_qn, war_mrb, war_mrbb], inc="tT")
                war_sqs[ck % 2] = [tok_nm]
                if ck == 0:
                    tok_red = W.do("vector", lambda e, run_off=run_off:
                                   e.reduce_max(rowS[0:1, run_off:run_off + 1],
                                                qn_ps, axis=AX.X),
                                   deps=[tok_nm], inc="vV")
                else:
                    W.do("vector", lambda e: e.reduce_max(
                        rowS[0:1, 8:9], qn_ps, axis=AX.X), deps=[tok_nm])
                    tok_red = W.do("vector", lambda e, run_off=run_off:
                                   e.tensor_max(rowS[0:1, run_off:run_off + 1],
                                                rowS[0:1, run_off:run_off + 1],
                                                rowS[0:1, 8:9]), inc="vV")
                war_qn = [tok_red]
        tok_p = W.do("vector", lambda e, h=h: e.tensor_mul(
            rowS[0:1, 9:10], rowS[0:1, 2 * h:2 * h + 1],
            rowS[0:1, 2 * h + 1:2 * h + 2]), inc="vV")
        W.do("scalar", lambda e: e.activation(
            rowS[0:1, 9:10], rowS[0:1, 9:10], AF.Sqrt, scale=1.0 / 128.0),
            deps=[tok_p])
        tok_ng = W.do("scalar", lambda e: e.mul(
            rowS[0:1, 10:11], rowS[0:1, 9:10], -1.0), inc="sS")
        tok_bc = W.do("tensor", lambda e: e.matmul(
            pp[0][:, 0:1], ones_r[:, :], rowS[0:1, 10:11],
            start=True, stop=True), deps=[tok_ng, war_sx], inc="tT")
        tok_nm2 = W.do("vector", lambda e, h=h: e.tensor_copy(
            H["negm_sb"][:, h:h + 1], pp[0][:, 0:1]), deps=[tok_bc], inc="vV")
        negm_toks.append(tok_nm2)

    o_sb, prst_sb, p_sb = H["o_sb"], H["prst_sb"], H["p_sb"]
    wp_sb, maskp_sb, rbs_sb = H["wp_sb"], H["maskp_sb"], H["rbs_sb"]
    rs_in, rs_out = H["rs_in"], H["rs_out"]

    tok_wp = W.do("gpsimd", lambda e: e.dma_start(
        out=wp_sb[:, :].rearrange("p (c n) -> p c n", n=2048),
        in_=H["wpj"].rearrange("(c p) n -> p c n", p=128)),
        deps=[ph1], inc="qG", n=16)
    tok_mk = W.do("sync", lambda e: e.dma_start(
        out=maskp_sb[:, :], in_=H["maskp"]), deps=[ph1], inc="qS", n=16)

    sums_ps = pp[3][0:1, 0:512]
    rb2_ps = pp[3][:, 512:1024]
    o_ps = [pp[2][:, 0:512], pp[2][:, 512:1024]]

    war_sc = [[negm_toks[1], tok_bc], [war_qn]]
    war_p = [[], []]
    war_o = [war_qkv[0], war_qkv[1]]
    war_sums = [war_vps[0], war_vps[1]]
    war_rb2 = [war_vps[2], war_vps[3]]
    war_prst = []
    war_rowE2 = [tok_bc]
    war_rowG = []
    war_rc2 = []
    war_rbs = []
    pair_idx = 0
    proj_slot_war = [[], [], [], []]
    qc_count = 0

    def emit_tail(h, g, ob, tok_sum_last):
        nonlocal war_sums, war_rowG, war_rc2, war_rb2, war_rbs
        tok_rg = W.do("scalar", lambda e: e.activation(
            rowG[0:1, 0:512], sums_ps, AF.Copy),
            deps=[tok_sum_last, war_rowG], inc="sS")
        W.do("scalar", lambda e: e.activation(
            H["rowLn"][0:1, 0:512], sums_ps, AF.Ln),
            deps=[tok_sum_last])
        tok_rc = W.do("scalar", lambda e: e.activation(
            H["rowR2"][0:1, 0:512], H["rowLn"][0:1, 0:512], AF.Exp,
            scale=-1.0), deps=[war_rc2], inc="sS")
        war_sums = [tok_rg, tok_rc]
        tok_fk = W.do("tensor", lambda e, h=h, ob=ob: e.matmul(
            o_ps[ob], H["bvrow_b"][0:1, h * 128:(h + 1) * 128],
            rowG[0:1, 0:512], start=False, stop=True),
            deps=[tok_rg, tok_bvb], inc="tT")
        war_rowG = [tok_fk]
        tok_b2 = W.do("tensor", lambda e: e.matmul(
            rb2_ps, H["ones_rb"][:, :], H["rowR2"][0:1, 0:512],
            start=True, stop=True),
            deps=[tok_rc, war_rb2, tok_orb], inc="tT")
        war_rc2 = [tok_b2]
        tok_cp = W.do("scalar", lambda e: e.activation(
            rbs_sb[:, :], rb2_ps, AF.Copy), deps=[tok_b2, war_rbs], inc="sS")
        war_rb2 = [tok_cp]
        tok_oe = W.do("vector", lambda e, h=h, g=g, ob=ob: e.tensor_tensor(
            o_sb[:, h * T + g * GQ:h * T + (g + 1) * GQ], o_ps[ob],
            rbs_sb[:, :], op=ALU.mult), deps=[tok_fk, tok_cp], inc="vV")
        war_rbs = [tok_oe]
        war_o[ob] = [tok_oe]
        return tok_oe, tok_fk

    tok_fk = tok_pm = tok_pe = tok_oe = None
    for g in range(16):
        oe_toks = []
        tail0 = None
        for h in range(2):
            npairs = 2 * g + 2
            ob = (2 * g + h) % 2
            tok_sum_last = None
            pend = None
            for pi in range(npairs + 1):
                if pi < npairs:
                    pb = pair_idx % 2
                    pair_idx += 1
                    sc = pp[pb]
                    kt0, kt1 = 2 * pi, 2 * pi + 1
                    W.do("tensor", lambda e, h=h, g=g, kt0=kt0, sc=sc:
                         e.matmul(
                        sc[:, 0:512],
                        K_sb[:, h * T + kt0 * 128:h * T + (kt0 + 1) * 128],
                        Q_sb[:, h * T + g * GQ:h * T + (g + 1) * GQ],
                        start=True, stop=True),
                        deps=[war_sc[pb], ph1])
                    tok_s2 = W.do("tensor", lambda e, h=h, g=g, kt1=kt1, sc=sc:
                                  e.matmul(
                        sc[:, 512:1024],
                        K_sb[:, h * T + kt1 * 128:h * T + (kt1 + 1) * 128],
                        Q_sb[:, h * T + g * GQ:h * T + (g + 1) * GQ],
                        start=True, stop=True), inc="tT")
                    tok_exp = W.do("scalar", lambda e, h=h, pb=pb, sc=sc:
                                   e.activation(p_sb[pb][:, :], sc[:, :],
                                                AF.Exp,
                                                bias=H["negm_sb"][:, h:h + 1],
                                                scale=SCALE),
                                   deps=[tok_s2, war_p[pb], negm_toks[1]],
                                   inc="sS")
                    war_sc[pb] = [tok_exp]
                    last_exp = tok_exp
                    gate_tok = tok_exp
                    if pi >= 2 * g:
                        rel = pi - 2 * g
                        gate_tok = W.do("vector", lambda e, pb=pb, rel=rel:
                                        e.tensor_tensor(
                                            p_sb[pb][:, :], p_sb[pb][:, :],
                                            maskp_sb[:, rel * 1024:
                                                     (rel + 1) * 1024],
                                            op=ALU.mult),
                                        deps=[tok_exp, tok_mk], inc="vV")
                    cur = (pb, gate_tok, kt0, kt1, pi)
                else:
                    cur = None
                if tail0 is not None and h == 1 and pi == 1:
                    oe0, _ = emit_tail(0, g, tail0[0], tail0[1])
                    oe_toks.append(oe0)
                    tail0 = None
                if pend is not None:
                    pb, gate_tok, kt0, kt1, ppi = pend
                    W.do("tensor", lambda e, pb=pb, kt0=kt0, h=h, ob=ob:
                         e.matmul(
                        o_ps[ob],
                        V_sb[:, kt0 * 256 + h * 128:kt0 * 256 + (h + 1) * 128],
                        p_sb[pb][:, 0:512], start=ppi == 0, stop=False),
                        deps=[gate_tok, war_o[ob]] if ppi == 0
                        else [gate_tok])
                    tok_av = W.do("tensor", lambda e, pb=pb, kt1=kt1, h=h,
                                  ob=ob: e.matmul(
                        o_ps[ob],
                        V_sb[:, kt1 * 256 + h * 128:kt1 * 256 + (h + 1) * 128],
                        p_sb[pb][:, 512:1024], start=False, stop=False))
                    W.do("tensor", lambda e, pb=pb: e.matmul(
                        sums_ps, ones_c[:, :], p_sb[pb][:, 0:512],
                        start=ppi == 0, stop=False),
                        deps=[war_sums] if ppi == 0 else ())
                    tok_sum_last = W.do("tensor", lambda e, pb=pb: e.matmul(
                        sums_ps, ones_c[:, :], p_sb[pb][:, 512:1024],
                        start=False, stop=ppi == npairs - 1), inc="tT")
                    war_p[pb] = [tok_sum_last]
                pend = cur
            if h == 0:
                tail0 = (ob, tok_sum_last)
                if g == 0:
                    oe0, _ = emit_tail(0, g, ob, tok_sum_last)
                    oe_toks.append(oe0)
                    tail0 = None
            else:
                tok_rg1 = W.do("scalar", lambda e: e.activation(
                    rowG[0:1, 0:512], sums_ps, AF.Copy),
                    deps=[tok_sum_last, war_rowG], inc="sS")
                W.do("scalar", lambda e: e.activation(
                    H["rowLn"][0:1, 0:512], sums_ps, AF.Ln),
                    deps=[tok_sum_last])
                tok_rc1 = W.do("scalar", lambda e: e.activation(
                    H["rowR2"][0:1, 0:512], H["rowLn"][0:1, 0:512], AF.Exp,
                    scale=-1.0), deps=[war_rc2], inc="sS")
                war_sums = [tok_rg1, tok_rc1]
                for cc in range(4):
                    slot = cc % 4
                    ps = pp[slot // 2][:, (slot % 2) * 512:(slot % 2 + 1) * 512]
                    W.do("tensor", lambda e, cc=cc, g=g, ps=ps: e.matmul(
                        ps, wp_sb[:, cc * 128:(cc + 1) * 128],
                        o_sb[:, g * GQ:(g + 1) * GQ], start=True, stop=False),
                        deps=[oe_toks[0], proj_slot_war[slot], tok_wp,
                              war_sc[slot // 2]])
                tok_fk1 = W.do("tensor", lambda e, h=h, ob=ob: e.matmul(
                    o_ps[ob], H["bvrow_b"][0:1, h * 128:(h + 1) * 128],
                    rowG[0:1, 0:512], start=False, stop=True),
                    deps=[tok_rg1, tok_bvb], inc="tT")
                war_rowG = [tok_fk1]
                tok_b21 = W.do("tensor", lambda e: e.matmul(
                    rb2_ps, H["ones_rb"][:, :], H["rowR2"][0:1, 0:512],
                    start=True, stop=True),
                    deps=[tok_rc1, war_rb2, tok_orb], inc="tT")
                war_rc2 = [tok_b21]
                tok_cp1 = W.do("scalar", lambda e: e.activation(
                    rbs_sb[:, :], rb2_ps, AF.Copy),
                    deps=[tok_b21, war_rbs], inc="sS")
                war_rb2 = [tok_cp1]
                tok_oe = W.do("vector", lambda e, h=h, g=g, ob=ob:
                              e.tensor_tensor(
                    o_sb[:, h * T + g * GQ:h * T + (g + 1) * GQ], o_ps[ob],
                    rbs_sb[:, :], op=ALU.mult), deps=[tok_fk1, tok_cp1],
                    inc="vV")
                war_rbs = [tok_oe]
                war_o[ob] = [tok_oe]
                oe_toks.append(tok_oe)
                tok_fk = tok_fk1
        half_toks = [[], []]
        for cc in range(16):
            slot = cc % 4
            ps = pp[slot // 2][:, (slot % 2) * 512:(slot % 2 + 1) * 512]
            if cc >= 4:
                W.do("tensor", lambda e, cc=cc, g=g, ps=ps: e.matmul(
                    ps, wp_sb[:, cc * 128:(cc + 1) * 128],
                    o_sb[:, g * GQ:(g + 1) * GQ], start=True, stop=False),
                    deps=[oe_toks[0], proj_slot_war[slot], tok_wp,
                          war_sc[slot // 2]])
            tok_pm = W.do("tensor", lambda e, cc=cc, g=g, ps=ps: e.matmul(
                ps, wp_sb[:, 2048 + cc * 128:2048 + (cc + 1) * 128],
                o_sb[:, T + g * GQ:T + (g + 1) * GQ],
                start=False, stop=True), deps=[oe_toks[1]], inc="tT")
            tok_pe = W.do("vector", lambda e, cc=cc, ps=ps: e.tensor_scalar(
                prst_sb[:, :].rearrange("p (tt c t) -> p tt c t",
                                        tt=4, c=16)[:, :, cc, :],
                ps.rearrange("p (a b) -> p a b", b=128),
                H["bproj_sb"][:, cc:cc + 1], None, op0=ALU.add),
                deps=[tok_pm, war_prst, tok_const], inc="vV")
            proj_slot_war[slot] = [tok_pe]
            half_toks[0].append(tok_pe)
        war_sc[0] = [war_sc[0], proj_slot_war[0], proj_slot_war[1]]
        war_sc[1] = [war_sc[1], proj_slot_war[2], proj_slot_war[3]]
        j, half = g // 2, g % 2
        tok_dma = W.do("scalar", lambda e, j=j, half=half: e.dma_start(
            out=rs_in[j].ap()[half * 4:(half + 1) * 4].rearrange(
                "r (c p2) t -> p2 (r c) t", p2=128),
            in_=prst_sb[:, :].rearrange("p (x t) -> p x t", t=128)),
            deps=[half_toks[0], war_prst], inc="qC", n=16)
        war_prst = [tok_dma]
        qc_count = tok_dma[1]
        if half == 1 and j < 7:
            W.do("gpsimd", lambda e, j=j: e.collective_compute(
                "ReduceScatter", ALU.add,
                replica_groups=[list(range(NC))],
                ins=[rs_in[j].ap().opt()],
                outs=[rs_out[j].ap().opt()]),
                deps=[("qC", qc_count)], inc="cc")

    ph2_t = [tok_pm, tok_fk]
    ph2_v = [tok_pe, tok_oe, war_rb2, war_sums]
    ph2 = [ph2_t, ph2_v, ("qC", qc_count), last_exp]
    ph2_nc = [ph2_t, ph2_v, last_exp]

    x2_sb, ln2_sb, hact_sb = H["x2_sb"], H["ln2_sb"], H["hact_sb"]
    wbuf, gtmp_sb = H["wbuf"], H["gtmp_sb"]

    xo_toks = []
    for j in range(8):
        tok = W.do("sync", lambda e, j=j: e.dma_start(
            out=x2_sb[:, :].rearrange("p (c t) -> p c t", t=1024)[
                :, :, j * 128:(j + 1) * 128],
            in_=H["xown"].rearrange("(c p) t -> p c t", p=128)[
                :, :, j * 128:(j + 1) * 128]),
            deps=[ph2_nc] if j == 0 else [xo_toks[j - 1]], inc="qS", n=16)
        xo_toks.append(tok)

    sx3_ps = pp[0][0:1, :]
    sq3_ps = pp[1][0:1, :]
    r2b_ps = pp[3]
    mr2b_ps = pp[2]
    sq_scr = hact_sb[:, 2048:4096]

    war_rsld2 = [[], []]
    war_pp0 = [ph2_nc]
    war_sqscr = []
    war_rowS3 = [war_rowE2]
    tok_sx3 = tok_sq3 = tok_q3 = None
    tok_rl6 = None
    tok_rl_prev = None
    tok_ln2 = None
    last_mul = None
    prev_cpR = prev_cpM = None
    for j in range(8):
        js = slice(j * 128, (j + 1) * 128)
        rsld_b = H["rsld2"][j % 2]
        rl_eng, rl_q = ("gpsimd", "qG") if j < 7 else ("scalar", "qC")
        deps_rl = [("cc", j + 1), war_rsld2[j % 2], tok_rl_prev]
        if j < 2:
            deps_rl += [ph2_v, last_exp]
        if j == 1:
            deps_rl += [("qC", qc_count)]
        tok_rl = W.do(rl_eng, lambda e, j=j, rb=rsld_b: e.dma_start(
            out=rb[:, :].rearrange("p (c t) -> p c t", t=128),
            in_=rs_out[j].ap().rearrange("(c p2) t -> p2 c t", p2=128)),
            deps=deps_rl, inc=rl_q, n=16)
        tok_rl_prev = tok_rl
        if j == 6:
            tok_rl6 = tok_rl
        if j == 1:
            W.do("gpsimd", lambda e: e.collective_compute(
                "ReduceScatter", ALU.add,
                replica_groups=[list(range(NC))],
                ins=[rs_in[7].ap().opt()],
                outs=[rs_out[7].ap().opt()]),
                deps=[("qC", qc_count)], inc="cc")
        tok_stt = W.do("vector", lambda e, j=j, rb=rsld_b: e.tensor_tensor(
            x2_sb[:, :].rearrange("p (c t) -> p c t", t=1024)[
                :, :, j * 128:(j + 1) * 128],
            x2_sb[:, :].rearrange("p (c t) -> p c t", t=1024)[
                :, :, j * 128:(j + 1) * 128],
            rb[:, :].rearrange("p (c t) -> p c t", t=128),
            op=ALU.add),
            deps=[tok_rl, xo_toks[j], ph2_v], inc="vV")
        war_rsld2[j % 2] = [tok_stt]
        tok_st1 = W.do("vector", lambda e, j=j: e.tensor_copy(
            ln2_sb[:, :].rearrange("p (c t) -> p c t", t=1024)[
                :, :, j * 128:(j + 1) * 128],
            x2_sb[:, :].rearrange("p (c t) -> p c t", t=1024)[
                :, :, j * 128:(j + 1) * 128]),
            deps=[ph2_t], inc="vV")
        for cc in range(16):
            tok_sx3 = W.do("tensor", lambda e, j=j, cc=cc: e.matmul(
                sx3_ps[:, js], ones_c[:, :],
                ln2_sb[:, cc * 1024 + j * 128:cc * 1024 + (j + 1) * 128],
                start=cc == 0, stop=cc == 15),
                deps=[tok_st1, war_pp0] if cc == 0 else (),
                inc="tT" if cc == 15 else None)
        tok_q3 = W.do("scalar", lambda e, j=j: e.activation(
            sq_scr[:, 0:2048].rearrange("p (c t) -> p c t", t=128),
            x2_sb[:, :].rearrange("p (c t) -> p c t", t=1024)[
                :, :, j * 128:(j + 1) * 128],
            AF.Square), deps=[tok_stt, ph2_nc, war_sqscr], inc="sS")
        for cc in range(16):
            tok_sq3 = W.do("tensor", lambda e, j=j, cc=cc: e.matmul(
                sq3_ps[:, js], ones_c[:, :],
                sq_scr[:, cc * 128:(cc + 1) * 128],
                start=cc == 0, stop=cc == 15),
                deps=[tok_q3, ph2_nc] if cc == 0 else (),
                inc="tT" if cc == 15 else None)
        war_sqscr = [tok_sq3]
        W.do("scalar", lambda e, js=js: e.mul(
            rowA[0:1, js], sx3_ps[0:1, js], 1.0 / 2048.0), deps=[tok_sx3])
        last_mul = W.do("scalar", lambda e, js=js: e.mul(
            rowB[0:1, js], sq3_ps[0:1, js], 1.0 / 2048.0),
            deps=[tok_sq3], inc="sS")
        war_pp0 = [last_mul]
        W.do("vector", lambda e, js=js: e.tensor_mul(
            rowS[0:1, js], rowA[0:1, js], rowA[0:1, js]),
            deps=[last_mul, war_rowS3])
        war_rowS3 = []
        tok_v3 = W.do("vector", lambda e, js=js: e.scalar_tensor_tensor(
            rowB[0:1, js], rowB[0:1, js], EPS, rowS[0:1, js],
            op0=ALU.add, op1=ALU.subtract), inc="vV")
        W.do("scalar", lambda e, js=js: e.activation(
            rowB[0:1, js], rowB[0:1, js], AF.Ln), deps=[tok_v3])
        tok_r3 = W.do("scalar", lambda e, js=js: e.activation(
            rowS[0:1, js], rowB[0:1, js], AF.Exp, scale=-0.5), inc="sS")
        tok_mr3 = W.do("vector", lambda e, js=js: e.tensor_mul(
            rowD[0:1, js], rowA[0:1, js], rowS[0:1, js]),
            deps=[tok_r3], inc="vV")
        tok_b3 = W.do("tensor", lambda e, js=js: e.matmul(
            r2b_ps[:, js], ones_r[:, :], rowS[0:1, js],
            start=True, stop=True),
            deps=([tok_r3, ph2_nc, war_sums, war_rb2] if j == 0
                  else [tok_r3, prev_cpR]), inc="tT")
        tok_m3 = W.do("tensor", lambda e, js=js: e.matmul(
            mr2b_ps[:, js], ones_r[:, :], rowD[0:1, js],
            start=True, stop=True),
            deps=([tok_mr3, ph2_nc] if j == 0 else [tok_mr3, prev_cpM]),
            inc="tT")
        prev_cpR = W.do("scalar", lambda e, js=js: e.activation(
            H["r2bb16"][:, js], r2b_ps[:, js], AF.Copy),
            deps=[tok_b3], inc="sS")
        prev_cpM = W.do("scalar", lambda e, js=js: e.activation(
            H["mr2bb16"][:, js], mr2b_ps[:, js], AF.Copy),
            deps=[tok_m3], inc="sS")
        W.do("vector", lambda e, j=j, js=js: e.tensor_tensor(
            ln2_sb[:, :].rearrange("p (c t) -> p c t", t=1024)[
                :, :, j * 128:(j + 1) * 128],
            x2_sb[:, :].rearrange("p (c t) -> p c t", t=1024)[
                :, :, j * 128:(j + 1) * 128],
            H["r2bb16"][:, js].unsqueeze(1).broadcast_to([128, 16, 128]),
            op=ALU.mult), deps=[prev_cpR, prev_cpM, tok_sx3])
        tok_ln2 = W.do("vector", lambda e, j=j, js=js: e.tensor_tensor(
            ln2_sb[:, :].rearrange("p (c t) -> p c t", t=1024)[
                :, :, j * 128:(j + 1) * 128],
            ln2_sb[:, :].rearrange("p (c t) -> p c t", t=1024)[
                :, :, j * 128:(j + 1) * 128],
            H["mr2bb16"][:, js].unsqueeze(1).broadcast_to([128, 16, 128]),
            op=ALU.subtract), inc="vV")

    mm_slots = [pp[0][:, 0:512], pp[0][:, 512:1024],
                pp[1][:, 0:512], pp[1][:, 512:1024]]
    war_wbuf = [[ph2], [ph2, war_rsld2[1], tok_rl_prev]]
    war_slot = [[last_mul], [last_mul], [last_mul], [last_mul]]
    war_hact = [ph2, tok_sq3]
    war_gtmp = [[ph2], [ph2]]
    war_x2 = [tok_ln2, tok_q3]
    sidx = 0
    tok_out_ready = None
    for p in range(NPASS):
        htoks = {}
        for stage in ("up", "gate", "down"):
            wb = wbuf[sidx % 2]
            wpar = sidx % 2
            sidx += 1
            if stage == "gate":
                src = H["gateT"]
            elif stage == "up":
                src = H["upT"]
            else:
                src = H["downT"]
            if stage in ("gate", "up"):
                dma_in = src[:, p * ISPAN:(p + 1) * ISPAN].rearrange(
                    "(c p2) n -> p2 c n", p2=128)
                dma_out = wb[:, :].rearrange("p (c n) -> p c n", c=16)
                nh = 512
            else:
                dma_in = src[p * ISPAN:(p + 1) * ISPAN, :].rearrange(
                    "(c p2) n -> p2 c n", p2=128)
                dma_out = wb[:, :].rearrange("p (c n) -> p c n", c=8)
                nh = 1024
            tok_wh = [None, None]
            tok_wdA = None
            for hf in range(2):
                tok_wd = W.do("gpsimd", lambda e, o=dma_out, i=dma_in,
                              hf=hf, nh=nh: e.dma_start(
                    out=o[:, :, hf * nh:(hf + 1) * nh],
                    in_=i[:, :, hf * nh:(hf + 1) * nh]),
                    deps=[war_wbuf[wpar], tok_rl6] if hf == 0 else [tok_wdA],
                    inc="qG", n=16)
                if hf == 0:
                    tok_wdA = tok_wd
                tok_wh[hf] = W.do("vector", lambda e, o=dma_out, hf=hf,
                                  nh=nh: e.tensor_scalar_add(
                    o[:, :, hf * nh:(hf + 1) * nh],
                    o[:, :, hf * nh:(hf + 1) * nh], -128.0),
                    deps=[tok_wd], inc="vV")
            tok_w = tok_wh
            if stage in ("gate", "up"):
                last_mm = None
                for ic in range(8):
                    gic = p * 8 + ic
                    ps2 = [mm_slots[(ic % 2) * 2], mm_slots[(ic % 2) * 2 + 1]]
                    tok_hm15 = [None, None]
                    for hm in range(16):
                        for tg in range(2):
                            tok_hm15[tg] = W.do("tensor", lambda e, hm=hm,
                                                ic=ic, tg=tg,
                                 wb=wb, ps2=ps2: e.matmul(
                                ps2[tg], wb[:, hm * 1024 + ic * 128:
                                            hm * 1024 + (ic + 1) * 128],
                                ln2_sb[:, hm * 1024 + tg * 512:
                                       hm * 1024 + (tg + 1) * 512],
                                start=hm == 0, stop=hm == 15),
                                deps=[tok_w[ic // 4],
                                      war_slot[(ic % 2) * 2 + tg],
                                      tok_ln2]
                                if hm == 0 else (),
                                inc="tT" if hm == 15 else None)
                    for tg in range(2):
                        slot = (ic % 2) * 2 + tg
                        ps = ps2[tg]
                        tok_mm = tok_hm15[tg]
                        last_mm = tok_mm
                        if stage == "up":
                            ev = W.do("scalar", lambda e, ic=ic, tg=tg,
                                      gic=gic, ps=ps: e.activation(
                                hact_sb[:, ic * 1024 + tg * 512:
                                        ic * 1024 + (tg + 1) * 512],
                                ps, AF.Identity,
                                scale=H["usc_sb"][:, gic:gic + 1]),
                                deps=[tok_mm, war_hact], inc="sS")
                            htoks[(ic, tg)] = ev
                            war_slot[slot] = [ev]
                        else:
                            gp = gtmp_sb[(ic * 2 + tg) % 2]
                            ev = W.do("scalar", lambda e, gp=gp, gic=gic,
                                      ps=ps: e.activation(
                                gp[:, :], ps, AF.Silu,
                                scale=H["gsc_sb"][:, gic:gic + 1]),
                                deps=[tok_mm,
                                      war_gtmp[(ic * 2 + tg) % 2]],
                                inc="sS")
                            war_slot[slot] = [ev]
                            tok_h = W.do("vector", lambda e, ic=ic, tg=tg,
                                         gp=gp: e.tensor_tensor(
                                hact_sb[:, ic * 1024 + tg * 512:
                                        ic * 1024 + (tg + 1) * 512],
                                hact_sb[:, ic * 1024 + tg * 512:
                                        ic * 1024 + (tg + 1) * 512],
                                gp[:, :], op=ALU.mult),
                                deps=[ev, htoks[(ic, tg)]], inc="vV")
                            war_gtmp[(ic * 2 + tg) % 2] = [tok_h]
                            htoks[(ic, tg)] = tok_h
                war_wbuf[wpar] = [last_mm]
            else:
                gate_done = [htoks[(ic, tg)]
                             for ic in range(8) for tg in range(2)]
                last_mm = None
                for cc in range(16):
                    ps2 = [mm_slots[(cc % 2) * 2], mm_slots[(cc % 2) * 2 + 1]]
                    tok_mms = [None, None]
                    for ic in range(8):
                        for tg in range(2):
                            tok_mms[tg] = W.do("tensor", lambda e, ic=ic,
                                               cc=cc, tg=tg, wb=wb, ps2=ps2:
                                               e.matmul(
                                ps2[tg], wb[:, ic * 2048 + cc * 128:
                                            ic * 2048 + (cc + 1) * 128],
                                hact_sb[:, ic * 1024 + tg * 512:
                                        ic * 1024 + (tg + 1) * 512],
                                start=ic == 0, stop=ic == 7),
                                deps=([tok_w[cc // 8],
                                       war_slot[(cc % 2) * 2 + tg]]
                                      + gate_done)
                                if ic == 0 else (),
                                inc="tT" if ic == 7 else None)
                    last_mm = tok_mms[1]
                    for tg in range(2):
                        ev = W.do("vector", lambda e, cc=cc, tg=tg,
                                  ps=ps2[tg]: e.scalar_tensor_tensor(
                            x2_sb[:, cc * 1024 + tg * 512:
                                  cc * 1024 + (tg + 1) * 512],
                            ps, H["dsc_sb"][:, cc:cc + 1],
                            x2_sb[:, cc * 1024 + tg * 512:
                                  cc * 1024 + (tg + 1) * 512],
                            op0=ALU.mult, op1=ALU.add),
                            deps=[tok_mms[tg], war_x2], inc="vV")
                        war_slot[(cc % 2) * 2 + tg] = [ev]
                        tok_out_ready = ev
                war_wbuf[wpar] = [last_mm]
                war_hact = [last_mm]

    tok_out = W.do("sync", lambda e: e.dma_start(
        out=H["out"].rearrange("(c p2) t -> p2 c t", p2=128),
        in_=x2_sb[:, :].rearrange("p (c t) -> p c t", t=1024)),
        deps=[tok_out_ready, ("vV", W.cnt["vV"])], inc="qS", n=16)
    W.wait_only("sync", [tok_out])
    W.wait_only("gpsimd", [tok_out])



_BF = ml_dtypes.bfloat16


def _masks_np():
    m = np.zeros((128, 2048), dtype=np.float32)
    q = np.arange(1024)
    for rel in range(2):
        for half in range(2):
            r = 2 * rel + half
            k = np.arange(128)[:, None]
            qq = q[None, half * 512:(half + 1) * 512] - half * 512
            m[:, rel * 1024 + half * 512: rel * 1024 + (half + 1) * 512] = (
                (128 * r + k) <= qq).astype(np.float32)
    return m.astype(_BF)


def kernel(L0_attn_x, L0_attn_w_attn, L0_attn_b_attn, L0_attn_w_proj,
           L0_attn_b_proj, L0_ffn_gate_w, L0_ffn_gate_scale, L0_ffn_up_w,
           L0_ffn_up_scale, L0_ffn_down_w, L0_ffn_down_scale):
    x = np.asarray(L0_attn_x, dtype=np.float32)[0]
    w_attn = np.asarray(L0_attn_w_attn, dtype=np.float32)
    b_attn = np.asarray(L0_attn_b_attn, dtype=np.float32)
    w_proj = np.asarray(L0_attn_w_proj, dtype=np.float32)
    b_proj = np.asarray(L0_attn_b_proj, dtype=np.float32)
    gate_w = np.asarray(L0_ffn_gate_w, dtype=np.int32)
    up_w = np.asarray(L0_ffn_up_w, dtype=np.int32)
    down_w = np.asarray(L0_ffn_down_w, dtype=np.int32)
    gate_s = np.asarray(L0_ffn_gate_scale, dtype=np.float32)
    up_s = np.asarray(L0_ffn_up_scale, dtype=np.float32)
    down_s = np.asarray(L0_ffn_down_scale, dtype=np.float32)

    xT = np.ascontiguousarray(x.T)
    gateT = np.ascontiguousarray(gate_w.T)
    upT = np.ascontiguousarray(up_w.T)
    downT = np.ascontiguousarray(down_w.T)
    gsc = np.ascontiguousarray(gate_s.reshape(64, 128).T)
    usc = np.ascontiguousarray(up_s.reshape(64, 128).T)
    dsc = np.ascontiguousarray(down_s.reshape(16, 128).T)
    bproj = np.ascontiguousarray(b_proj.reshape(16, 128).T)
    masks = _masks_np()
    ones_c = np.ones((128, 1), dtype=_BF)
    ones_r = np.ones((1, 128), dtype=np.float32)

    in_maps = []
    for k in range(NC):
        cols = np.concatenate([
            w_attn[:, 256 * k:256 * (k + 1)],
            w_attn[:, 2048 + 256 * k:2048 + 256 * (k + 1)],
            w_attn[:, 4096 + 256 * k:4096 + 256 * (k + 1)]], axis=1)
        bq = b_attn[256 * k:256 * (k + 1)]
        bk = b_attn[2048 + 256 * k:2048 + 256 * (k + 1)]
        bv = b_attn[4096 + 256 * k:4096 + 256 * (k + 1)]
        bcol = np.stack([bq[:128], bq[128:], bk[:128], bk[128:]], axis=1)
        own_rows = np.concatenate(
            [x[1024 * j + 128 * k:1024 * j + 128 * (k + 1)] for j in range(8)])
        in_maps.append(dict(
            xfeat=xT,
            xown=np.ascontiguousarray(own_rows.T),
            watt=np.ascontiguousarray(cols),
            bcol=np.ascontiguousarray(bcol),
            bvrow=np.ascontiguousarray(bv.reshape(1, 256)),
            wpj=np.ascontiguousarray(w_proj[256 * k:256 * (k + 1), :]),
            bproj=bproj if k == 0 else np.zeros_like(bproj),
            gateT=gateT, upT=upT, downT=downT,
            gsc=gsc, usc=usc, dsc=dsc,
            maskp=masks, onescol=ones_c, onesrow=ones_r,
        ))

    nc = build_program()
    res = run_bass_kernel_spmd(nc, in_maps, core_ids=list(range(NC)))

    full = np.empty((T, C), dtype=np.float32)
    for k in range(NC):
        o = res.results[k]["out"]
        for j in range(8):
            full[1024 * j + 128 * k:1024 * j + 128 * (k + 1), :] = \
                o[:, 128 * j:128 * (j + 1)].T
    return full.reshape(1, T, C)


# version 32
# speedup vs baseline: 1.0015x; 1.0015x over previous
import math
from contextlib import ExitStack

import numpy as np
import ml_dtypes

import concourse.bass as bass
import concourse.mybir as mybir
from concourse.bass_utils import run_bass_kernel_spmd

F32 = mybir.dt.float32
BF16 = mybir.dt.bfloat16
I32 = mybir.dt.int32
AF = mybir.ActivationFunctionType
ALU = mybir.AluOpType
AX = mybir.AxisListType

T = 8192
C = 2048
D = 128
FF = 8192
NC = 8
TOWN = 1024
NG = 16
GQ = 512
ISPAN = 1024
NPASS = FF // ISPAN
SCALE = 1.0 / math.sqrt(D)
EPS = 1e-5

PRODUCERS = {
    "tT": "tensor", "sS": "scalar", "vV": "vector",
    "qS": "", "qC": "", "qG": "", "qV": "", "cc": "",
}


class Walk:

    def __init__(self, nc, which, eng, sems):
        self.nc = nc
        self.which = which
        self.eng = eng
        self.sems = sems
        self.cnt = {k: 0 for k in sems}
        self.last_wait = {k: 0 for k in sems}

    def _wait(self, deps):
        for dep in deps:
            if dep is None:
                continue
            if isinstance(dep, list):
                self._wait(dep)
                continue
            s, v = dep
            if PRODUCERS[s] == self.which:
                continue
            if self.last_wait[s] >= v:
                continue
            self.eng.wait_ge(self.sems[s], v)
            self.last_wait[s] = v

    def do(self, engine, fn, deps=(), inc=None, n=1):
        tok = None
        if inc is not None:
            self.cnt[inc] += n
            tok = (inc, self.cnt[inc])
        if self.which == engine:
            self._wait(deps)
            ins = fn(self.eng)
            if inc is not None:
                ins.then_inc(self.sems[inc], n)
        return tok

    def wait_only(self, engine, deps):
        if self.which == engine:
            self._wait(deps)


def build_program():
    nc = bass.Bass("TRN2", target_bir_lowering=False, debug=False,
                   detect_race_conditions=False)

    H = {}
    H["nc"] = nc
    def inp(name, shape, dtype):
        H[name] = nc.dram_tensor(name, shape, dtype, kind="ExternalInput").ap()
    inp("xfeat", [C, T], F32)
    inp("xown", [C, TOWN], F32)
    inp("watt", [C, 768], F32)
    inp("bcol", [128, 4], F32)
    inp("bvrow", [1, 256], F32)
    inp("wpj", [256, C], F32)
    inp("bproj", [128, 16], F32)
    inp("gateT", [C, FF], I32)
    inp("upT", [C, FF], I32)
    inp("downT", [FF, C], I32)
    inp("gsc", [128, 64], F32)
    inp("usc", [128, 64], F32)
    inp("dsc", [128, 16], F32)
    inp("maskp", [128, 2048], BF16)
    inp("onescol", [128, 1], BF16)
    inp("onesrow", [1, 128], F32)
    H["out"] = nc.dram_tensor("out", [C, TOWN], F32, kind="ExternalOutput").ap()

    H["rs_in"] = [nc.dram_tensor(f"rs_in{j}", [NC, C, 128], F32, kind="Internal")
                  for j in range(8)]
    H["rs_out"] = [nc.dram_tensor(f"rs_out{j}", [C, 128], F32, kind="Internal")
                   for j in range(8)]

    with ExitStack() as ctx:
        e = ctx.enter_context
        base = (nc.sbuf_base + 31) & ~31
        assert nc.sbuf_top - base >= 206 * 1024, (nc.sbuf_base, nc.sbuf_top)

        def at(name, shape, dtype, off_kb):
            t = nc.alloc_sbuf_tensor_at(name, shape, dtype,
                                        offset=base + int(off_kb * 1024))
            H[name] = t
            return t

        at("ones_c", [128, 1], BF16, 0.0)
        at("ones_r", [1, 128], F32, 0.03125)
        at("bcol_sb", [128, 4], F32, 0.59375)
        at("bvrow_f", [1, 256], F32, 0.625)
        at("bvrow_b", [1, 256], BF16, 1.625)
        at("gsc_sb", [128, 64], F32, 2.125)
        at("usc_sb", [128, 64], F32, 2.375)
        at("dsc_sb", [128, 16], F32, 2.625)
        at("bproj_sb", [128, 16], F32, 2.6875)
        at("negm_sb", [128, 2], F32, 2.75)
        at("ones_rb", [1, 128], BF16, 2.78125)
        at("rowA", [1, 1024], F32, 4.0)
        at("rowB", [1, 1024], F32, 8.0)
        at("rbb16", [128, 512], BF16, 10.0)
        at("mrb16", [128, 512], BF16, 14.0)
        at("rowD", [1, 1024], F32, 12.0)
        at("rowS", [1, 1024], F32, 16.0)
        at("rowW", [1, 768], F32, 20.0)

        OV = 23.0
        at("Q_sb", [128, 2 * T], BF16, OV)
        at("K_sb", [128, 2 * T], BF16, OV + 32)
        at("V_sb", [128, 64 * 256], BF16, OV + 64)
        H["xc_sb"] = [at(f"xc{i}", [128, 16 * GQ], BF16, OV + 96 + 16 * i)
                      for i in range(2)]
        H["xq_sb"] = [at(f"xq{i}", [128, 16 * GQ], BF16, OV + 128 + 16 * i)
                      for i in range(2)]
        at("watt_sb", [128, 16 * 768], BF16, OV + 160)
        at("o_sb", [128, 2 * T], BF16, OV + 96)
        at("prst_sb", [128, 16 * GQ], F32, OV + 128)
        H["p_sb"] = [at(f"p{i}", [128, 1024], BF16, OV + 160 + 2 * i)
                     for i in range(2)]
        at("wp_sb", [128, 2 * C], BF16, OV + 164)
        at("maskp_sb", [128, 2048], BF16, OV + 172)
        at("rbs_sb", [128, 512], F32, OV + 176)
        H["sqs_sb"] = [at(f"sqs{i}", [128, 512], BF16, OV + 178 + i)
                       for i in range(2)]
        at("rowG", [1, 512], BF16, 4.0)
        at("rowR2", [1, 512], BF16, 5.0)
        at("rowLn", [1, 512], F32, 6.0)
        at("x2_sb", [128, 16 * 1024], F32, OV)
        at("ln2_sb", [128, 16 * 1024], BF16, OV + 64)
        H["wbuf"] = [at(f"wbuf{i}", [128, 16384], BF16, OV + 96 + 32 * i)
                     for i in range(2)]
        at("hact_sb", [128, 8 * 1024], BF16, OV + 160)
        H["gtmp_sb"] = [at(f"gtmp{i}", [128, 512], BF16, OV + 176 + i)
                        for i in range(2)]
        at("r2bb16", [128, 1024], BF16, OV + 160)
        at("mr2bb16", [128, 1024], BF16, OV + 162)
        H["rsld2"] = [at("rsld_a", [128, 16 * 128], F32, OV + 172),
                      at("rsld_b", [128, 16 * 128], F32, OV + 152)]

        H["pp"] = [e(nc.psum_tensor(f"pp{i}", [128, 1024], F32))
                   for i in range(4)]

        sems = {name: e(nc.semaphore(name)) for name in PRODUCERS}
        block = e(nc.Block())

        def run(which):
            def body(eng):
                emit_all(Walk(nc, which, eng, sems), H)
            return body

        block.sync(run("sync"))
        block.tensor(run("tensor"))
        block.scalar(run("scalar"))
        block.vector(run("vector"))
        block.gpsimd(run("gpsimd"))

    return nc



def emit_all(W, H):
    pp = H["pp"]
    ones_c, ones_r = H["ones_c"], H["ones_r"]
    rowA, rowB = H["rowA"], H["rowB"]
    rowD, rowS, rowW, rowG = H["rowD"], H["rowS"], H["rowW"], H["rowG"]
    Q_sb, K_sb, V_sb, watt_sb = H["Q_sb"], H["K_sb"], H["V_sb"], H["watt_sb"]
    xc_sb, xq_sb = H["xc_sb"], H["xq_sb"]

    toks = []
    for dname, sname in (("bcol_sb", "bcol"), ("bvrow_f", "bvrow"),
                         ("gsc_sb", "gsc"), ("usc_sb", "usc"),
                         ("dsc_sb", "dsc"), ("bproj_sb", "bproj"),
                         ("ones_r", "onesrow"), ("ones_c", "onescol")):
        dst, src = H[dname], H[sname]
        full = tuple(slice(None) for _ in dst.shape)
        toks.append(W.do("sync", lambda e, d=dst, s=src, f=full: e.dma_start(
            out=d[f], in_=s), inc="qS", n=16))
    tok_const = toks[-1]
    tok_bvb = W.do("vector", lambda e: e.tensor_copy(
        H["bvrow_b"][:, :], H["bvrow_f"][:, :]), deps=[toks[-1]], inc="vV")
    tok_orb = W.do("gpsimd", lambda e: e.dma_start(
        out=H["ones_rb"][:, :], in_=H["onesrow"]), inc="qG", n=16)
    tok_watt = W.do("gpsimd", lambda e: e.dma_start(
        out=watt_sb[:, :].rearrange("p (c n) -> p c n", n=768),
        in_=H["watt"].rearrange("(c p) n -> p c n", p=128)), inc="qG", n=16)

    sx_ps = pp[0][0:1, 0:512]
    sq_ps = pp[0][0:1, 512:1024]
    rb_ps = pp[1][:, 0:512]
    mrb_ps = pp[1][:, 512:1024]
    qkv_ps = [pp[2][:, 0:512], pp[2][:, 512:1024]]
    v_ps = [pp[3][:, 0:256], pp[3][:, 512:768]]

    war_xc_h = [[[], []], [[], []]]
    war_xq = [[], []]
    war_sx = [tok_watt]
    war_rb = []
    war_rbb = []
    war_mrb = []
    war_mrbb = []
    war_qkv = [[], [], [], []]
    war_vps = [[], [], [], []]
    war_rowA = []
    war_rowB = []
    war_rowD = []
    last_p1_evac = None
    last_p1_mm = None

    def p1_prep(g):
        b = g % 2
        xcg, xqg = xc_sb[b], xq_sb[b]
        tok_ld = W.do("gpsimd", lambda e, g=g, xcg=xcg: e.dma_start(
            out=xcg[:, :].rearrange("p (c t) -> p c t", t=GQ),
            in_=H["xfeat"][:, g * GQ:(g + 1) * GQ].rearrange(
                "(c p) t -> p c t", p=128)),
            deps=[war_xc_h[b][0], war_xc_h[b][1]], inc="qG", n=16)
        tok_sx = None
        for hm in range(16):
            tok_sx = W.do("tensor", lambda e, hm=hm, xcg=xcg: e.matmul(
                sx_ps, ones_c[:, :], xcg[:, hm * GQ:(hm + 1) * GQ],
                start=hm == 0, stop=hm == 15),
                deps=[tok_ld, war_sx, tok_const] if hm == 0 else (),
                inc="tT" if hm == 15 else None)
        sq_toks = []
        for sb_i in range(4):
            tok = W.do("scalar", lambda e, sb_i=sb_i, xcg=xcg, xqg=xqg:
                       e.activation(xqg[:, sb_i * 2048:(sb_i + 1) * 2048],
                                    xcg[:, sb_i * 2048:(sb_i + 1) * 2048],
                                    AF.Square),
                       deps=[tok_ld, war_xq[b]] if sb_i == 0 else (),
                       inc="sS")
            sq_toks.append(tok)
        tok_sq = None
        for hm in range(16):
            tok_sq = W.do("tensor", lambda e, hm=hm, xqg=xqg: e.matmul(
                sq_ps, ones_c[:, :], xqg[:, hm * GQ:(hm + 1) * GQ],
                start=hm == 0, stop=hm == 15),
                deps=[sq_toks[hm // 4]] if hm % 4 == 0 else (),
                inc="tT" if hm == 15 else None)
        W.do("scalar", lambda e: e.mul(rowA[:, 0:512], sx_ps, 1.0 / 2048.0),
             deps=[tok_sx, war_rowA])
        tok_st = W.do("scalar", lambda e: e.mul(
            rowB[:, 0:512], sq_ps, 1.0 / 2048.0),
            deps=[tok_sq, war_rowB], inc="sS")
        return tok_st, tok_sx, tok_sq

    def p1_statsvec(tok_st, tok_rb_prev):
        W.do("vector", lambda e: e.tensor_mul(
            rowS[:, 0:512], rowA[:, 0:512], rowA[:, 0:512]),
            deps=[tok_st, tok_rb_prev])
        W.do("vector", lambda e: e.scalar_tensor_tensor(
            rowB[:, 0:512], rowB[:, 0:512], EPS, rowS[:, 0:512],
            op0=ALU.add, op1=ALU.subtract))
        tok_vv = W.do("vector", lambda e: e.tensor_copy(
            rowB[:, 0:512], rowB[:, 0:512]), inc="vV")
        W.do("scalar", lambda e: e.activation(
            rowB[:, 0:512], rowB[:, 0:512], AF.Ln), deps=[tok_vv])
        tok_r = W.do("scalar", lambda e: e.activation(
            rowS[:, 0:512], rowB[:, 0:512], AF.Exp, scale=-0.5), inc="sS")
        tok_mr = W.do("vector", lambda e: e.tensor_mul(
            rowD[:, 0:512], rowA[:, 0:512], rowS[:, 0:512]),
            deps=[tok_r, war_rowD], inc="vV")
        return tok_r, tok_mr

    st0, sx0, sq0 = p1_prep(0)
    prep = {"st": st0, "sx": sx0, "sq": sq0}
    tok_r, tok_mr = p1_statsvec(st0, None)

    for g in range(NG):
        b = g % 2
        xcg, xqg = xc_sb[b], xq_sb[b]
        cur_st, cur_sx, cur_sq = prep["st"], prep["sx"], prep["sq"]
        cur_r, cur_mr = tok_r, tok_mr
        war_sx = [cur_st]
        war_rowA = [cur_mr]
        war_rowB = [cur_r]
        tok_rb = W.do("tensor", lambda e: e.matmul(
            rb_ps, ones_r[:, :], rowS[0:1, 0:512], start=True, stop=True),
            deps=[cur_r, war_rb, tok_const], inc="tT")
        tok_mb = W.do("tensor", lambda e: e.matmul(
            mrb_ps, ones_r[:, :], rowD[0:1, 0:512], start=True, stop=True),
            deps=[cur_mr, war_mrb], inc="tT")
        war_rowD = [tok_mb]
        tok_rbc = W.do("vector", lambda e: e.tensor_copy(
            H["rbb16"][:, :], rb_ps), deps=[tok_rb, war_rbb], inc="vV")
        tok_mbc = W.do("vector", lambda e: e.tensor_copy(
            H["mrb16"][:, :], mrb_ps), deps=[tok_mb, war_mrbb], inc="vV")
        tt_toks = []
        for hm in range(16):
            W.do("vector", lambda e, hm=hm, xcg=xcg, xqg=xqg: e.tensor_tensor(
                xqg[:, hm * GQ:(hm + 1) * GQ],
                xcg[:, hm * GQ:(hm + 1) * GQ],
                H["rbb16"][:, :], op=ALU.mult),
                deps=[cur_sq] if hm == 0 else ())
            tok = W.do("vector", lambda e, hm=hm, xqg=xqg: e.tensor_tensor(
                xqg[:, hm * GQ:(hm + 1) * GQ],
                xqg[:, hm * GQ:(hm + 1) * GQ],
                H["mrb16"][:, :], op=ALU.subtract), inc="vV")
            tt_toks.append(tok)
        if g + 1 < NG:
            nst, nsx, nsq = p1_prep(g + 1)
            prep = {"st": nst, "sx": nsx, "sq": nsq}
            tok_r, tok_mr = p1_statsvec(prep["st"], tok_rb)
        qk_slots = [pp[2][:, 0:512], pp[2][:, 512:1024],
                    rb_ps, mrb_ps]
        v_slots = [pp[3][:, 0:256], pp[3][:, 512:768]]
        qk_last = [None] * 4
        v_last = [None] * 4
        for hm in range(16):
            for cc in range(4):
                qk_last[cc] = W.do("tensor", lambda e, hm=hm, cc=cc,
                                   xqg=xqg: e.matmul(
                    qk_slots[cc], watt_sb[:, hm * 768 + cc * 128:
                                          hm * 768 + (cc + 1) * 128],
                    xqg[:, hm * GQ:(hm + 1) * GQ],
                    start=hm == 0, stop=hm == 15),
                    deps=[tt_toks[hm], war_qkv[cc],
                          tok_rbc if cc == 2 else None,
                          tok_mbc if cc == 3 else None],
                    inc="tT" if hm == 15 else None)
            for tt in range(2):
                v_last[tt] = W.do("tensor", lambda e, hm=hm, tt=tt,
                                  xqg=xqg: e.matmul(
                    v_slots[tt], xqg[:, hm * GQ + tt * 128:
                                     hm * GQ + (tt + 1) * 128],
                    watt_sb[:, hm * 768 + 512: hm * 768 + 768],
                    start=hm == 0, stop=hm == 15),
                    deps=[tt_toks[hm], war_vps[tt]],
                    inc="tT" if hm == 15 else None)
        for tt in range(2):
            kt = 4 * g + tt
            ev = W.do("scalar", lambda e, kt=kt, tt=tt: e.activation(
                V_sb[:, kt * 256:(kt + 1) * 256], v_slots[tt], AF.Identity),
                deps=[v_last[tt]], inc="sS")
            war_vps[tt] = [ev]
        for hm in range(16):
            for tt in range(2, 4):
                v_last[tt] = W.do("tensor", lambda e, hm=hm, tt=tt,
                                  xqg=xqg: e.matmul(
                    v_slots[tt - 2], xqg[:, hm * GQ + tt * 128:
                                         hm * GQ + (tt + 1) * 128],
                    watt_sb[:, hm * 768 + 512: hm * 768 + 768],
                    start=hm == 0, stop=hm == 15),
                    deps=[war_vps[tt - 2], war_vps[tt]] if hm == 0 else (),
                    inc="tT" if hm == 15 else None)
        for cc in range(4):
            dst = Q_sb if cc < 2 else K_sb
            hh = cc % 2
            ev = W.do("scalar", lambda e, dst=dst, hh=hh, g=g, cc=cc:
                      e.activation(dst[:, hh * T + g * GQ:
                                       hh * T + (g + 1) * GQ],
                                   qk_slots[cc], AF.Identity,
                                   bias=H["bcol_sb"][:, cc:cc + 1]),
                      deps=[qk_last[cc]], inc="sS")
            war_qkv[cc] = [ev]
        war_rb = [war_qkv[2]]
        war_mrb = [war_qkv[3]]
        for tt in range(2, 4):
            kt = 4 * g + tt
            ev = W.do("scalar", lambda e, kt=kt, tt=tt: e.activation(
                V_sb[:, kt * 256:(kt + 1) * 256], v_slots[tt - 2],
                AF.Identity),
                deps=[v_last[tt]], inc="sS")
            war_vps[tt] = [ev]
            last_p1_evac = ev
        last_p1_mm = v_last[3]
        war_xq[b] = [v_last[3], tt_toks[15]]
        war_rbb = [tt_toks[15]]
        war_mrbb = [tt_toks[15]]
        war_xc_h[b][0] = [tt_toks[7], cur_sx]
        war_xc_h[b][1] = [tt_toks[15], cur_sx]

    ph1 = [last_p1_evac, last_p1_mm]

    sqs_sb = H["sqs_sb"]
    qn_ps = pp[1][0:1, 512:1024]
    negm_toks = []
    war_sqs = [[ph1], [ph1]]
    war_qn = []
    tok_bc = None
    for h in range(2):
        for si, src in enumerate((Q_sb, K_sb)):
            run_off = h * 2 + si
            for ck in range(16):
                sb = sqs_sb[ck % 2]
                tok_q2 = W.do("scalar", lambda e, src=src, h=h, ck=ck, sb=sb:
                              e.activation(sb[:, :],
                                           src[:, h * T + ck * 512:
                                               h * T + (ck + 1) * 512],
                                           AF.Square),
                              deps=[ph1, war_sqs[ck % 2]], inc="sS")
                tok_nm = W.do("tensor", lambda e, sb=sb: e.matmul(
                    qn_ps, ones_c[:, :], sb[:, :], start=True, stop=True),
                    deps=[tok_q2, war_qn, war_mrb, war_mrbb], inc="tT")
                war_sqs[ck % 2] = [tok_nm]
                if ck == 0:
                    tok_red = W.do("vector", lambda e, run_off=run_off:
                                   e.reduce_max(rowS[0:1, run_off:run_off + 1],
                                                qn_ps, axis=AX.X),
                                   deps=[tok_nm], inc="vV")
                else:
                    W.do("vector", lambda e: e.reduce_max(
                        rowS[0:1, 8:9], qn_ps, axis=AX.X), deps=[tok_nm])
                    tok_red = W.do("vector", lambda e, run_off=run_off:
                                   e.tensor_max(rowS[0:1, run_off:run_off + 1],
                                                rowS[0:1, run_off:run_off + 1],
                                                rowS[0:1, 8:9]), inc="vV")
                war_qn = [tok_red]
        tok_p = W.do("vector", lambda e, h=h: e.tensor_mul(
            rowS[0:1, 9:10], rowS[0:1, 2 * h:2 * h + 1],
            rowS[0:1, 2 * h + 1:2 * h + 2]), inc="vV")
        W.do("scalar", lambda e: e.activation(
            rowS[0:1, 9:10], rowS[0:1, 9:10], AF.Sqrt, scale=1.0 / 128.0),
            deps=[tok_p])
        tok_ng = W.do("scalar", lambda e: e.mul(
            rowS[0:1, 10:11], rowS[0:1, 9:10], -1.0), inc="sS")
        tok_bc = W.do("tensor", lambda e: e.matmul(
            pp[0][:, 0:1], ones_r[:, :], rowS[0:1, 10:11],
            start=True, stop=True), deps=[tok_ng, war_sx], inc="tT")
        tok_nm2 = W.do("vector", lambda e, h=h: e.tensor_copy(
            H["negm_sb"][:, h:h + 1], pp[0][:, 0:1]), deps=[tok_bc], inc="vV")
        negm_toks.append(tok_nm2)

    o_sb, prst_sb, p_sb = H["o_sb"], H["prst_sb"], H["p_sb"]
    wp_sb, maskp_sb, rbs_sb = H["wp_sb"], H["maskp_sb"], H["rbs_sb"]
    rs_in, rs_out = H["rs_in"], H["rs_out"]

    tok_wp = W.do("gpsimd", lambda e: e.dma_start(
        out=wp_sb[:, :].rearrange("p (c n) -> p c n", n=2048),
        in_=H["wpj"].rearrange("(c p) n -> p c n", p=128)),
        deps=[ph1], inc="qG", n=16)
    tok_mk = W.do("sync", lambda e: e.dma_start(
        out=maskp_sb[:, :], in_=H["maskp"]), deps=[ph1], inc="qS", n=16)

    sums_ps = pp[3][0:1, 0:512]
    rb2_ps = pp[3][:, 512:1024]
    o_ps = [pp[2][:, 0:512], pp[2][:, 512:1024]]

    war_sc = [[negm_toks[1], tok_bc], [war_qn]]
    war_p = [[], []]
    war_o = [war_qkv[0], war_qkv[1]]
    war_sums = [war_vps[0], war_vps[1]]
    war_rb2 = [war_vps[2], war_vps[3]]
    war_prst = []
    war_rowE2 = [tok_bc]
    war_rowG = []
    war_rc2 = []
    war_rbs = []
    pair_idx = 0
    proj_slot_war = [[], [], [], []]
    qc_count = 0

    def emit_tail(h, g, ob, tok_sum_last):
        nonlocal war_sums, war_rowG, war_rc2, war_rb2, war_rbs
        tok_rg = W.do("scalar", lambda e: e.activation(
            rowG[0:1, 0:512], sums_ps, AF.Copy),
            deps=[tok_sum_last, war_rowG], inc="sS")
        W.do("scalar", lambda e: e.activation(
            H["rowLn"][0:1, 0:512], sums_ps, AF.Ln),
            deps=[tok_sum_last])
        tok_rc = W.do("scalar", lambda e: e.activation(
            H["rowR2"][0:1, 0:512], H["rowLn"][0:1, 0:512], AF.Exp,
            scale=-1.0), deps=[war_rc2], inc="sS")
        war_sums = [tok_rg, tok_rc]
        tok_fk = W.do("tensor", lambda e, h=h, ob=ob: e.matmul(
            o_ps[ob], H["bvrow_b"][0:1, h * 128:(h + 1) * 128],
            rowG[0:1, 0:512], start=False, stop=True),
            deps=[tok_rg, tok_bvb], inc="tT")
        war_rowG = [tok_fk]
        tok_b2 = W.do("tensor", lambda e: e.matmul(
            rb2_ps, H["ones_rb"][:, :], H["rowR2"][0:1, 0:512],
            start=True, stop=True),
            deps=[tok_rc, war_rb2, tok_orb], inc="tT")
        war_rc2 = [tok_b2]
        tok_cp = W.do("scalar", lambda e: e.activation(
            rbs_sb[:, :], rb2_ps, AF.Copy), deps=[tok_b2, war_rbs], inc="sS")
        war_rb2 = [tok_cp]
        tok_oe = W.do("vector", lambda e, h=h, g=g, ob=ob: e.tensor_tensor(
            o_sb[:, h * T + g * GQ:h * T + (g + 1) * GQ], o_ps[ob],
            rbs_sb[:, :], op=ALU.mult), deps=[tok_fk, tok_cp], inc="vV")
        war_rbs = [tok_oe]
        war_o[ob] = [tok_oe]
        return tok_oe, tok_fk

    tok_fk = tok_pm = tok_pe = tok_oe = None
    for g in range(16):
        oe_toks = []
        tail0 = None
        for h in range(2):
            npairs = 2 * g + 2
            ob = (2 * g + h) % 2
            tok_sum_last = None
            pend = None
            for pi in range(npairs + 1):
                if pi < npairs:
                    pb = pair_idx % 2
                    pair_idx += 1
                    sc = pp[pb]
                    kt0, kt1 = 2 * pi, 2 * pi + 1
                    W.do("tensor", lambda e, h=h, g=g, kt0=kt0, sc=sc:
                         e.matmul(
                        sc[:, 0:512],
                        K_sb[:, h * T + kt0 * 128:h * T + (kt0 + 1) * 128],
                        Q_sb[:, h * T + g * GQ:h * T + (g + 1) * GQ],
                        start=True, stop=True),
                        deps=[war_sc[pb], ph1])
                    tok_s2 = W.do("tensor", lambda e, h=h, g=g, kt1=kt1, sc=sc:
                                  e.matmul(
                        sc[:, 512:1024],
                        K_sb[:, h * T + kt1 * 128:h * T + (kt1 + 1) * 128],
                        Q_sb[:, h * T + g * GQ:h * T + (g + 1) * GQ],
                        start=True, stop=True), inc="tT")
                    tok_exp = W.do("scalar", lambda e, h=h, pb=pb, sc=sc:
                                   e.activation(p_sb[pb][:, :], sc[:, :],
                                                AF.Exp,
                                                bias=H["negm_sb"][:, h:h + 1],
                                                scale=SCALE),
                                   deps=[tok_s2, war_p[pb], negm_toks[1]],
                                   inc="sS")
                    war_sc[pb] = [tok_exp]
                    last_exp = tok_exp
                    gate_tok = tok_exp
                    if pi >= 2 * g:
                        rel = pi - 2 * g
                        gate_tok = W.do("vector", lambda e, pb=pb, rel=rel:
                                        e.tensor_tensor(
                                            p_sb[pb][:, :], p_sb[pb][:, :],
                                            maskp_sb[:, rel * 1024:
                                                     (rel + 1) * 1024],
                                            op=ALU.mult),
                                        deps=[tok_exp, tok_mk], inc="vV")
                    cur = (pb, gate_tok, kt0, kt1, pi)
                else:
                    cur = None
                if tail0 is not None and h == 1 and pi == 1:
                    oe0, _ = emit_tail(0, g, tail0[0], tail0[1])
                    oe_toks.append(oe0)
                    tail0 = None
                if pend is not None:
                    pb, gate_tok, kt0, kt1, ppi = pend
                    W.do("tensor", lambda e, pb=pb, kt0=kt0, h=h, ob=ob:
                         e.matmul(
                        o_ps[ob],
                        V_sb[:, kt0 * 256 + h * 128:kt0 * 256 + (h + 1) * 128],
                        p_sb[pb][:, 0:512], start=ppi == 0, stop=False),
                        deps=[gate_tok, war_o[ob]] if ppi == 0
                        else [gate_tok])
                    tok_av = W.do("tensor", lambda e, pb=pb, kt1=kt1, h=h,
                                  ob=ob: e.matmul(
                        o_ps[ob],
                        V_sb[:, kt1 * 256 + h * 128:kt1 * 256 + (h + 1) * 128],
                        p_sb[pb][:, 512:1024], start=False, stop=False))
                    W.do("tensor", lambda e, pb=pb: e.matmul(
                        sums_ps, ones_c[:, :], p_sb[pb][:, 0:512],
                        start=ppi == 0, stop=False),
                        deps=[war_sums] if ppi == 0 else ())
                    tok_sum_last = W.do("tensor", lambda e, pb=pb: e.matmul(
                        sums_ps, ones_c[:, :], p_sb[pb][:, 512:1024],
                        start=False, stop=ppi == npairs - 1), inc="tT")
                    war_p[pb] = [tok_sum_last]
                pend = cur
            if h == 0:
                tail0 = (ob, tok_sum_last)
                if g == 0:
                    oe0, _ = emit_tail(0, g, ob, tok_sum_last)
                    oe_toks.append(oe0)
                    tail0 = None
            else:
                tok_rg1 = W.do("scalar", lambda e: e.activation(
                    rowG[0:1, 0:512], sums_ps, AF.Copy),
                    deps=[tok_sum_last, war_rowG], inc="sS")
                W.do("scalar", lambda e: e.activation(
                    H["rowLn"][0:1, 0:512], sums_ps, AF.Ln),
                    deps=[tok_sum_last])
                tok_rc1 = W.do("scalar", lambda e: e.activation(
                    H["rowR2"][0:1, 0:512], H["rowLn"][0:1, 0:512], AF.Exp,
                    scale=-1.0), deps=[war_rc2], inc="sS")
                war_sums = [tok_rg1, tok_rc1]
                for cc in range(4):
                    slot = cc % 4
                    ps = pp[slot // 2][:, (slot % 2) * 512:(slot % 2 + 1) * 512]
                    W.do("tensor", lambda e, cc=cc, g=g, ps=ps: e.matmul(
                        ps, wp_sb[:, cc * 128:(cc + 1) * 128],
                        o_sb[:, g * GQ:(g + 1) * GQ], start=True, stop=False),
                        deps=[oe_toks[0], proj_slot_war[slot], tok_wp,
                              war_sc[slot // 2]])
                tok_fk1 = W.do("tensor", lambda e, h=h, ob=ob: e.matmul(
                    o_ps[ob], H["bvrow_b"][0:1, h * 128:(h + 1) * 128],
                    rowG[0:1, 0:512], start=False, stop=True),
                    deps=[tok_rg1, tok_bvb], inc="tT")
                war_rowG = [tok_fk1]
                tok_b21 = W.do("tensor", lambda e: e.matmul(
                    rb2_ps, H["ones_rb"][:, :], H["rowR2"][0:1, 0:512],
                    start=True, stop=True),
                    deps=[tok_rc1, war_rb2, tok_orb], inc="tT")
                war_rc2 = [tok_b21]
                tok_cp1 = W.do("scalar", lambda e: e.activation(
                    rbs_sb[:, :], rb2_ps, AF.Copy),
                    deps=[tok_b21, war_rbs], inc="sS")
                war_rb2 = [tok_cp1]
                tok_oe = W.do("vector", lambda e, h=h, g=g, ob=ob:
                              e.tensor_tensor(
                    o_sb[:, h * T + g * GQ:h * T + (g + 1) * GQ], o_ps[ob],
                    rbs_sb[:, :], op=ALU.mult), deps=[tok_fk1, tok_cp1],
                    inc="vV")
                war_rbs = [tok_oe]
                war_o[ob] = [tok_oe]
                oe_toks.append(tok_oe)
                tok_fk = tok_fk1
        half_toks = [[], []]
        for cc in range(16):
            slot = cc % 4
            ps = pp[slot // 2][:, (slot % 2) * 512:(slot % 2 + 1) * 512]
            if cc >= 4:
                W.do("tensor", lambda e, cc=cc, g=g, ps=ps: e.matmul(
                    ps, wp_sb[:, cc * 128:(cc + 1) * 128],
                    o_sb[:, g * GQ:(g + 1) * GQ], start=True, stop=False),
                    deps=[oe_toks[0], proj_slot_war[slot], tok_wp,
                          war_sc[slot // 2]])
            tok_pm = W.do("tensor", lambda e, cc=cc, g=g, ps=ps: e.matmul(
                ps, wp_sb[:, 2048 + cc * 128:2048 + (cc + 1) * 128],
                o_sb[:, T + g * GQ:T + (g + 1) * GQ],
                start=False, stop=True), deps=[oe_toks[1]], inc="tT")
            tok_pe = W.do("vector", lambda e, cc=cc, ps=ps: e.tensor_scalar(
                prst_sb[:, :].rearrange("p (tt c t) -> p tt c t",
                                        tt=4, c=16)[:, :, cc, :],
                ps.rearrange("p (a b) -> p a b", b=128),
                H["bproj_sb"][:, cc:cc + 1], None, op0=ALU.add),
                deps=[tok_pm, war_prst, tok_const], inc="vV")
            proj_slot_war[slot] = [tok_pe]
            half_toks[0].append(tok_pe)
        war_sc[0] = [war_sc[0], proj_slot_war[0], proj_slot_war[1]]
        war_sc[1] = [war_sc[1], proj_slot_war[2], proj_slot_war[3]]
        j, half = g // 2, g % 2
        tok_dma = W.do("scalar", lambda e, j=j, half=half: e.dma_start(
            out=rs_in[j].ap()[half * 4:(half + 1) * 4].rearrange(
                "r (c p2) t -> p2 (r c) t", p2=128),
            in_=prst_sb[:, :].rearrange("p (x t) -> p x t", t=128)),
            deps=[half_toks[0], war_prst], inc="qC", n=16)
        war_prst = [tok_dma]
        qc_count = tok_dma[1]
        if half == 1 and j < 7:
            W.do("gpsimd", lambda e, j=j: e.collective_compute(
                "ReduceScatter", ALU.add,
                replica_groups=[list(range(NC))],
                ins=[rs_in[j].ap().opt()],
                outs=[rs_out[j].ap().opt()]),
                deps=[("qC", qc_count)], inc="cc")

    ph2_t = [tok_pm, tok_fk]
    ph2_v = [tok_pe, tok_oe, war_rb2, war_sums]
    ph2 = [ph2_t, ph2_v, ("qC", qc_count), last_exp]
    ph2_nc = [ph2_t, ph2_v, last_exp]

    x2_sb, ln2_sb, hact_sb = H["x2_sb"], H["ln2_sb"], H["hact_sb"]
    wbuf, gtmp_sb = H["wbuf"], H["gtmp_sb"]

    xo_toks = []
    for j in range(8):
        tok = W.do("sync", lambda e, j=j: e.dma_start(
            out=x2_sb[:, :].rearrange("p (c t) -> p c t", t=1024)[
                :, :, j * 128:(j + 1) * 128],
            in_=H["xown"].rearrange("(c p) t -> p c t", p=128)[
                :, :, j * 128:(j + 1) * 128]),
            deps=[ph2_nc] if j == 0 else [xo_toks[j - 1]], inc="qS", n=16)
        xo_toks.append(tok)

    sx3_ps = pp[0][0:1, :]
    sq3_ps = pp[1][0:1, :]
    r2b_ps = pp[3]
    mr2b_ps = pp[2]
    sq_scr = hact_sb[:, 2048:4096]

    war_rsld2 = [[], []]
    war_pp0 = [ph2_nc]
    war_sqscr = []
    war_rowS3 = [war_rowE2]
    tok_sx3 = tok_sq3 = tok_q3 = None
    tok_rl6 = None
    tok_rl_prev = None
    tok_ln2 = None
    last_mul = None
    prev_cpR = prev_cpM = None
    for j in range(8):
        js = slice(j * 128, (j + 1) * 128)
        rsld_b = H["rsld2"][j % 2]
        rl_eng, rl_q = ("gpsimd", "qG") if j < 7 else ("scalar", "qC")
        deps_rl = [("cc", j + 1), war_rsld2[j % 2], tok_rl_prev]
        if j < 2:
            deps_rl += [ph2_v, last_exp]
        if j == 1:
            deps_rl += [("qC", qc_count)]
        tok_rl = W.do(rl_eng, lambda e, j=j, rb=rsld_b: e.dma_start(
            out=rb[:, :].rearrange("p (c t) -> p c t", t=128),
            in_=rs_out[j].ap().rearrange("(c p2) t -> p2 c t", p2=128)),
            deps=deps_rl, inc=rl_q, n=16)
        tok_rl_prev = tok_rl
        if j == 6:
            tok_rl6 = tok_rl
        if j == 1:
            W.do("gpsimd", lambda e: e.collective_compute(
                "ReduceScatter", ALU.add,
                replica_groups=[list(range(NC))],
                ins=[rs_in[7].ap().opt()],
                outs=[rs_out[7].ap().opt()]),
                deps=[("qC", qc_count)], inc="cc")
        tok_stt = W.do("vector", lambda e, j=j, rb=rsld_b: e.tensor_tensor(
            x2_sb[:, :].rearrange("p (c t) -> p c t", t=1024)[
                :, :, j * 128:(j + 1) * 128],
            x2_sb[:, :].rearrange("p (c t) -> p c t", t=1024)[
                :, :, j * 128:(j + 1) * 128],
            rb[:, :].rearrange("p (c t) -> p c t", t=128),
            op=ALU.add),
            deps=[tok_rl, xo_toks[j], ph2_v], inc="vV")
        war_rsld2[j % 2] = [tok_stt]
        tok_st1 = W.do("vector", lambda e, j=j: e.tensor_copy(
            ln2_sb[:, :].rearrange("p (c t) -> p c t", t=1024)[
                :, :, j * 128:(j + 1) * 128],
            x2_sb[:, :].rearrange("p (c t) -> p c t", t=1024)[
                :, :, j * 128:(j + 1) * 128]),
            deps=[ph2_t], inc="vV")
        for cc in range(16):
            tok_sx3 = W.do("tensor", lambda e, j=j, cc=cc: e.matmul(
                sx3_ps[:, js], ones_c[:, :],
                ln2_sb[:, cc * 1024 + j * 128:cc * 1024 + (j + 1) * 128],
                start=cc == 0, stop=cc == 15),
                deps=[tok_st1, war_pp0] if cc == 0 else (),
                inc="tT" if cc == 15 else None)
        tok_q3 = W.do("scalar", lambda e, j=j: e.activation(
            sq_scr[:, 0:2048].rearrange("p (c t) -> p c t", t=128),
            x2_sb[:, :].rearrange("p (c t) -> p c t", t=1024)[
                :, :, j * 128:(j + 1) * 128],
            AF.Square), deps=[tok_stt, ph2_nc, war_sqscr], inc="sS")
        for cc in range(16):
            tok_sq3 = W.do("tensor", lambda e, j=j, cc=cc: e.matmul(
                sq3_ps[:, js], ones_c[:, :],
                sq_scr[:, cc * 128:(cc + 1) * 128],
                start=cc == 0, stop=cc == 15),
                deps=[tok_q3, ph2_nc] if cc == 0 else (),
                inc="tT" if cc == 15 else None)
        war_sqscr = [tok_sq3]
        W.do("scalar", lambda e, js=js: e.mul(
            rowA[0:1, js], sx3_ps[0:1, js], 1.0 / 2048.0), deps=[tok_sx3])
        last_mul = W.do("scalar", lambda e, js=js: e.mul(
            rowB[0:1, js], sq3_ps[0:1, js], 1.0 / 2048.0),
            deps=[tok_sq3], inc="sS")
        war_pp0 = [last_mul]
        W.do("vector", lambda e, js=js: e.tensor_mul(
            rowS[0:1, js], rowA[0:1, js], rowA[0:1, js]),
            deps=[last_mul, war_rowS3])
        war_rowS3 = []
        tok_v3 = W.do("vector", lambda e, js=js: e.scalar_tensor_tensor(
            rowB[0:1, js], rowB[0:1, js], EPS, rowS[0:1, js],
            op0=ALU.add, op1=ALU.subtract), inc="vV")
        W.do("scalar", lambda e, js=js: e.activation(
            rowB[0:1, js], rowB[0:1, js], AF.Ln), deps=[tok_v3])
        tok_r3 = W.do("scalar", lambda e, js=js: e.activation(
            rowS[0:1, js], rowB[0:1, js], AF.Exp, scale=-0.5), inc="sS")
        tok_mr3 = W.do("vector", lambda e, js=js: e.tensor_mul(
            rowD[0:1, js], rowA[0:1, js], rowS[0:1, js]),
            deps=[tok_r3], inc="vV")
        tok_b3 = W.do("tensor", lambda e, js=js: e.matmul(
            r2b_ps[:, js], ones_r[:, :], rowS[0:1, js],
            start=True, stop=True),
            deps=([tok_r3, ph2_nc, war_sums, war_rb2] if j == 0
                  else [tok_r3, prev_cpR]), inc="tT")
        tok_m3 = W.do("tensor", lambda e, js=js: e.matmul(
            mr2b_ps[:, js], ones_r[:, :], rowD[0:1, js],
            start=True, stop=True),
            deps=([tok_mr3, ph2_nc] if j == 0 else [tok_mr3, prev_cpM]),
            inc="tT")
        prev_cpR = W.do("scalar", lambda e, js=js: e.activation(
            H["r2bb16"][:, js], r2b_ps[:, js], AF.Copy),
            deps=[tok_b3], inc="sS")
        prev_cpM = W.do("scalar", lambda e, js=js: e.activation(
            H["mr2bb16"][:, js], mr2b_ps[:, js], AF.Copy),
            deps=[tok_m3], inc="sS")
        W.do("vector", lambda e, j=j, js=js: e.tensor_tensor(
            ln2_sb[:, :].rearrange("p (c t) -> p c t", t=1024)[
                :, :, j * 128:(j + 1) * 128],
            x2_sb[:, :].rearrange("p (c t) -> p c t", t=1024)[
                :, :, j * 128:(j + 1) * 128],
            H["r2bb16"][:, js].unsqueeze(1).broadcast_to([128, 16, 128]),
            op=ALU.mult), deps=[prev_cpR, prev_cpM, tok_sx3])
        tok_ln2 = W.do("vector", lambda e, j=j, js=js: e.tensor_tensor(
            ln2_sb[:, :].rearrange("p (c t) -> p c t", t=1024)[
                :, :, j * 128:(j + 1) * 128],
            ln2_sb[:, :].rearrange("p (c t) -> p c t", t=1024)[
                :, :, j * 128:(j + 1) * 128],
            H["mr2bb16"][:, js].unsqueeze(1).broadcast_to([128, 16, 128]),
            op=ALU.subtract), inc="vV")

    mm_slots = [pp[0][:, 0:512], pp[0][:, 512:1024],
                pp[1][:, 0:512], pp[1][:, 512:1024]]
    war_wbuf = [[ph2], [ph2, war_rsld2[1], tok_rl_prev]]
    war_slot = [[last_mul], [last_mul], [last_mul], [last_mul]]
    war_hact = [ph2, tok_sq3]
    war_gtmp = [[ph2], [ph2]]
    war_x2 = [tok_ln2, tok_q3]
    prev_wd = None
    sidx = 0
    tok_out_ready = None
    for p in range(NPASS):
        htoks = {}
        for stage in ("up", "gate", "down"):
            wb = wbuf[sidx % 2]
            wpar = sidx % 2
            sidx += 1
            if stage == "gate":
                src = H["gateT"]
            elif stage == "up":
                src = H["upT"]
            else:
                src = H["downT"]
            if stage in ("gate", "up"):
                dma_in = src[:, p * ISPAN:(p + 1) * ISPAN].rearrange(
                    "(c p2) n -> p2 c n", p2=128)
                dma_out = wb[:, :].rearrange("p (c n) -> p c n", c=16)
                nh = 512
            else:
                dma_in = src[p * ISPAN:(p + 1) * ISPAN, :].rearrange(
                    "(c p2) n -> p2 c n", p2=128)
                dma_out = wb[:, :].rearrange("p (c n) -> p c n", c=8)
                nh = 1024
            if True:
                tok_wh = [None, None]
                for hf in range(2):
                    tok_wd = W.do("gpsimd", lambda e, o=dma_out, i=dma_in,
                                  hf=hf, nh=nh: e.dma_start(
                        out=o[:, :, hf * nh:(hf + 1) * nh],
                        in_=i[:, :, hf * nh:(hf + 1) * nh]),
                        deps=[war_wbuf[wpar], tok_rl6, prev_wd]
                        if hf == 0 else [prev_wd],
                        inc="qG", n=16)
                    prev_wd = tok_wd
                    tok_wh[hf] = W.do("vector", lambda e, o=dma_out, hf=hf,
                                      nh=nh: e.tensor_scalar_add(
                        o[:, :, hf * nh:(hf + 1) * nh],
                        o[:, :, hf * nh:(hf + 1) * nh], -128.0),
                        deps=[tok_wd], inc="vV")
                tok_w = tok_wh
            if stage in ("gate", "up"):
                last_mm = None
                for ic in range(8):
                    gic = p * 8 + ic
                    ps2 = [mm_slots[(ic % 2) * 2], mm_slots[(ic % 2) * 2 + 1]]
                    tok_hm15 = [None, None]
                    for hm in range(16):
                        for tg in range(2):
                            tok_hm15[tg] = W.do("tensor", lambda e, hm=hm,
                                                ic=ic, tg=tg,
                                 wb=wb, ps2=ps2: e.matmul(
                                ps2[tg], wb[:, hm * 1024 + ic * 128:
                                            hm * 1024 + (ic + 1) * 128],
                                ln2_sb[:, hm * 1024 + tg * 512:
                                       hm * 1024 + (tg + 1) * 512],
                                start=hm == 0, stop=hm == 15),
                                deps=[tok_w[ic // 4],
                                      war_slot[(ic % 2) * 2 + tg],
                                      tok_ln2]
                                if hm == 0 else (),
                                inc="tT" if hm == 15 else None)
                    for tg in range(2):
                        slot = (ic % 2) * 2 + tg
                        ps = ps2[tg]
                        tok_mm = tok_hm15[tg]
                        last_mm = tok_mm
                        if stage == "up":
                            ev = W.do("scalar", lambda e, ic=ic, tg=tg,
                                      gic=gic, ps=ps: e.activation(
                                hact_sb[:, ic * 1024 + tg * 512:
                                        ic * 1024 + (tg + 1) * 512],
                                ps, AF.Identity,
                                scale=H["usc_sb"][:, gic:gic + 1]),
                                deps=[tok_mm, war_hact], inc="sS")
                            htoks[(ic, tg)] = ev
                            war_slot[slot] = [ev]
                        else:
                            gp = gtmp_sb[(ic * 2 + tg) % 2]
                            ev = W.do("scalar", lambda e, gp=gp, gic=gic,
                                      ps=ps: e.activation(
                                gp[:, :], ps, AF.Silu,
                                scale=H["gsc_sb"][:, gic:gic + 1]),
                                deps=[tok_mm,
                                      war_gtmp[(ic * 2 + tg) % 2]],
                                inc="sS")
                            war_slot[slot] = [ev]
                            tok_h = W.do("vector", lambda e, ic=ic, tg=tg,
                                         gp=gp: e.tensor_tensor(
                                hact_sb[:, ic * 1024 + tg * 512:
                                        ic * 1024 + (tg + 1) * 512],
                                hact_sb[:, ic * 1024 + tg * 512:
                                        ic * 1024 + (tg + 1) * 512],
                                gp[:, :], op=ALU.mult),
                                deps=[ev, htoks[(ic, tg)]], inc="vV")
                            war_gtmp[(ic * 2 + tg) % 2] = [tok_h]
                            htoks[(ic, tg)] = tok_h
                war_wbuf[wpar] = [last_mm]
            else:
                gate_done = [htoks[(ic, tg)]
                             for ic in range(8) for tg in range(2)]
                last_mm = None
                for cc in range(16):
                    ps2 = [mm_slots[(cc % 2) * 2], mm_slots[(cc % 2) * 2 + 1]]
                    tok_mms = [None, None]
                    for ic in range(8):
                        for tg in range(2):
                            tok_mms[tg] = W.do("tensor", lambda e, ic=ic,
                                               cc=cc, tg=tg, wb=wb, ps2=ps2:
                                               e.matmul(
                                ps2[tg], wb[:, ic * 2048 + cc * 128:
                                            ic * 2048 + (cc + 1) * 128],
                                hact_sb[:, ic * 1024 + tg * 512:
                                        ic * 1024 + (tg + 1) * 512],
                                start=ic == 0, stop=ic == 7),
                                deps=([tok_w[cc // 8],
                                       war_slot[(cc % 2) * 2 + tg]]
                                      + gate_done)
                                if ic == 0 else (),
                                inc="tT" if ic == 7 else None)
                    last_mm = tok_mms[1]
                    for tg in range(2):
                        ev = W.do("vector", lambda e, cc=cc, tg=tg,
                                  ps=ps2[tg]: e.scalar_tensor_tensor(
                            x2_sb[:, cc * 1024 + tg * 512:
                                  cc * 1024 + (tg + 1) * 512],
                            ps, H["dsc_sb"][:, cc:cc + 1],
                            x2_sb[:, cc * 1024 + tg * 512:
                                  cc * 1024 + (tg + 1) * 512],
                            op0=ALU.mult, op1=ALU.add),
                            deps=[tok_mms[tg], war_x2], inc="vV")
                        war_slot[(cc % 2) * 2 + tg] = [ev]
                        tok_out_ready = ev
                war_wbuf[wpar] = [last_mm]
                war_hact = [last_mm]

    tok_out = W.do("sync", lambda e: e.dma_start(
        out=H["out"].rearrange("(c p2) t -> p2 c t", p2=128),
        in_=x2_sb[:, :].rearrange("p (c t) -> p c t", t=1024)),
        deps=[tok_out_ready, ("vV", W.cnt["vV"])], inc="qS", n=16)
    W.wait_only("sync", [tok_out])
    W.wait_only("gpsimd", [tok_out])



_BF = ml_dtypes.bfloat16


def _masks_np():
    m = np.zeros((128, 2048), dtype=np.float32)
    q = np.arange(1024)
    for rel in range(2):
        for half in range(2):
            r = 2 * rel + half
            k = np.arange(128)[:, None]
            qq = q[None, half * 512:(half + 1) * 512] - half * 512
            m[:, rel * 1024 + half * 512: rel * 1024 + (half + 1) * 512] = (
                (128 * r + k) <= qq).astype(np.float32)
    return m.astype(_BF)


def kernel(L0_attn_x, L0_attn_w_attn, L0_attn_b_attn, L0_attn_w_proj,
           L0_attn_b_proj, L0_ffn_gate_w, L0_ffn_gate_scale, L0_ffn_up_w,
           L0_ffn_up_scale, L0_ffn_down_w, L0_ffn_down_scale):
    x = np.asarray(L0_attn_x, dtype=np.float32)[0]
    w_attn = np.asarray(L0_attn_w_attn, dtype=np.float32)
    b_attn = np.asarray(L0_attn_b_attn, dtype=np.float32)
    w_proj = np.asarray(L0_attn_w_proj, dtype=np.float32)
    b_proj = np.asarray(L0_attn_b_proj, dtype=np.float32)
    gate_w = np.asarray(L0_ffn_gate_w, dtype=np.int32)
    up_w = np.asarray(L0_ffn_up_w, dtype=np.int32)
    down_w = np.asarray(L0_ffn_down_w, dtype=np.int32)
    gate_s = np.asarray(L0_ffn_gate_scale, dtype=np.float32)
    up_s = np.asarray(L0_ffn_up_scale, dtype=np.float32)
    down_s = np.asarray(L0_ffn_down_scale, dtype=np.float32)

    xT = np.ascontiguousarray(x.T)
    gateT = np.ascontiguousarray(gate_w.T)
    upT = np.ascontiguousarray(up_w.T)
    downT = np.ascontiguousarray(down_w.T)
    gsc = np.ascontiguousarray(gate_s.reshape(64, 128).T)
    usc = np.ascontiguousarray(up_s.reshape(64, 128).T)
    dsc = np.ascontiguousarray(down_s.reshape(16, 128).T)
    bproj = np.ascontiguousarray(b_proj.reshape(16, 128).T)
    masks = _masks_np()
    ones_c = np.ones((128, 1), dtype=_BF)
    ones_r = np.ones((1, 128), dtype=np.float32)

    in_maps = []
    for k in range(NC):
        cols = np.concatenate([
            w_attn[:, 256 * k:256 * (k + 1)],
            w_attn[:, 2048 + 256 * k:2048 + 256 * (k + 1)],
            w_attn[:, 4096 + 256 * k:4096 + 256 * (k + 1)]], axis=1)
        bq = b_attn[256 * k:256 * (k + 1)]
        bk = b_attn[2048 + 256 * k:2048 + 256 * (k + 1)]
        bv = b_attn[4096 + 256 * k:4096 + 256 * (k + 1)]
        bcol = np.stack([bq[:128], bq[128:], bk[:128], bk[128:]], axis=1)
        own_rows = np.concatenate(
            [x[1024 * j + 128 * k:1024 * j + 128 * (k + 1)] for j in range(8)])
        in_maps.append(dict(
            xfeat=xT,
            xown=np.ascontiguousarray(own_rows.T),
            watt=np.ascontiguousarray(cols),
            bcol=np.ascontiguousarray(bcol),
            bvrow=np.ascontiguousarray(bv.reshape(1, 256)),
            wpj=np.ascontiguousarray(w_proj[256 * k:256 * (k + 1), :]),
            bproj=bproj if k == 0 else np.zeros_like(bproj),
            gateT=gateT, upT=upT, downT=downT,
            gsc=gsc, usc=usc, dsc=dsc,
            maskp=masks, onescol=ones_c, onesrow=ones_r,
        ))

    nc = build_program()
    res = run_bass_kernel_spmd(nc, in_maps, core_ids=list(range(NC)))

    full = np.empty((T, C), dtype=np.float32)
    for k in range(NC):
        o = res.results[k]["out"]
        for j in range(8):
            full[1024 * j + 128 * k:1024 * j + 128 * (k + 1), :] = \
                o[:, 128 * j:128 * (j + 1)].T
    return full.reshape(1, T, C)


# version 33 (final)
# speedup vs baseline: 1.0291x; 1.0275x over previous
import math
from contextlib import ExitStack

import numpy as np
import ml_dtypes

import concourse.bass as bass
import concourse.mybir as mybir
from concourse.bass_utils import run_bass_kernel_spmd

F32 = mybir.dt.float32
BF16 = mybir.dt.bfloat16
I32 = mybir.dt.int32
AF = mybir.ActivationFunctionType
ALU = mybir.AluOpType
AX = mybir.AxisListType

T = 8192
C = 2048
D = 128
FF = 8192
NC = 8
TOWN = 1024
NG = 16
GQ = 512
ISPAN = 1024
NPASS = FF // ISPAN
SCALE = 1.0 / math.sqrt(D)
EPS = 1e-5

PRODUCERS = {
    "tT": "tensor", "sS": "scalar", "vV": "vector",
    "qS": "", "qC": "", "qG": "", "qV": "", "cc": "",
}


class Walk:

    def __init__(self, nc, which, eng, sems):
        self.nc = nc
        self.which = which
        self.eng = eng
        self.sems = sems
        self.cnt = {k: 0 for k in sems}
        self.last_wait = {k: 0 for k in sems}

    def _wait(self, deps):
        for dep in deps:
            if dep is None:
                continue
            if isinstance(dep, list):
                self._wait(dep)
                continue
            s, v = dep
            if PRODUCERS[s] == self.which:
                continue
            if self.last_wait[s] >= v:
                continue
            self.eng.wait_ge(self.sems[s], v)
            self.last_wait[s] = v

    def do(self, engine, fn, deps=(), inc=None, n=1):
        tok = None
        if inc is not None:
            self.cnt[inc] += n
            tok = (inc, self.cnt[inc])
        if self.which == engine:
            self._wait(deps)
            ins = fn(self.eng)
            if inc is not None:
                ins.then_inc(self.sems[inc], n)
        return tok

    def wait_only(self, engine, deps):
        if self.which == engine:
            self._wait(deps)


def build_program():
    nc = bass.Bass("TRN2", target_bir_lowering=False, debug=False,
                   detect_race_conditions=False)

    H = {}
    H["nc"] = nc
    def inp(name, shape, dtype):
        H[name] = nc.dram_tensor(name, shape, dtype, kind="ExternalInput").ap()
    inp("xfeat", [C, T], F32)
    inp("xown", [C, TOWN], F32)
    inp("watt", [C, 768], F32)
    inp("bcol", [128, 4], F32)
    inp("bvrow", [1, 256], F32)
    inp("wpj", [256, C], F32)
    inp("bproj", [128, 16], F32)
    inp("gateT", [C, FF], I32)
    inp("upT", [C, FF], I32)
    inp("downT", [FF, C], I32)
    inp("gsc", [128, 64], F32)
    inp("usc", [128, 64], F32)
    inp("dsc", [128, 16], F32)
    inp("maskp", [128, 2048], BF16)
    inp("onescol", [128, 1], BF16)
    inp("onesrow", [1, 128], F32)
    H["out"] = nc.dram_tensor("out", [C, TOWN], F32, kind="ExternalOutput").ap()

    H["rs_in"] = [nc.dram_tensor(f"rs_in{j}", [NC, C, 128], F32, kind="Internal")
                  for j in range(8)]
    H["rs_out"] = [nc.dram_tensor(f"rs_out{j}", [C, 128], F32, kind="Internal")
                   for j in range(8)]

    with ExitStack() as ctx:
        e = ctx.enter_context
        base = (nc.sbuf_base + 31) & ~31
        assert nc.sbuf_top - base >= 206 * 1024, (nc.sbuf_base, nc.sbuf_top)

        def at(name, shape, dtype, off_kb):
            t = nc.alloc_sbuf_tensor_at(name, shape, dtype,
                                        offset=base + int(off_kb * 1024))
            H[name] = t
            return t

        at("ones_c", [128, 1], BF16, 0.0)
        at("ones_r", [1, 128], F32, 0.03125)
        at("bcol_sb", [128, 4], F32, 0.59375)
        at("bvrow_f", [1, 256], F32, 0.625)
        at("bvrow_b", [1, 256], BF16, 1.625)
        at("gsc_sb", [128, 64], F32, 2.125)
        at("usc_sb", [128, 64], F32, 2.375)
        at("dsc_sb", [128, 16], F32, 2.625)
        at("bproj_sb", [128, 16], F32, 2.6875)
        at("negm_sb", [128, 2], F32, 2.75)
        at("ones_rb", [1, 128], BF16, 2.78125)
        at("rowA", [1, 1024], F32, 4.0)
        at("rowB", [1, 1024], F32, 8.0)
        at("rbb16", [128, 512], BF16, 10.0)
        at("mrb16", [128, 512], BF16, 14.0)
        at("rowD", [1, 1024], F32, 12.0)
        at("rowS", [1, 1024], F32, 16.0)
        at("rowW", [1, 768], F32, 20.0)

        OV = 23.0
        at("Q_sb", [128, 2 * T], BF16, OV)
        at("K_sb", [128, 2 * T], BF16, OV + 32)
        at("V_sb", [128, 64 * 256], BF16, OV + 64)
        H["xc_sb"] = [at(f"xc{i}", [128, 16 * GQ], BF16, OV + 96 + 16 * i)
                      for i in range(2)]
        H["xq_sb"] = [at(f"xq{i}", [128, 16 * GQ], BF16, OV + 128 + 16 * i)
                      for i in range(2)]
        at("watt_sb", [128, 16 * 768], BF16, OV + 160)
        at("o_sb", [128, 2 * T], BF16, OV + 96)
        at("prst_sb", [128, 16 * GQ], F32, OV + 128)
        H["p_sb"] = [at(f"p{i}", [128, 1024], BF16, OV + 160 + 2 * i)
                     for i in range(2)]
        at("wp_sb", [128, 2 * C], BF16, OV + 164)
        at("maskp_sb", [128, 2048], BF16, OV + 172)
        at("rbs_sb", [128, 512], F32, OV + 176)
        H["sqs_sb"] = [at(f"sqs{i}", [128, 512], BF16, OV + 178 + i)
                       for i in range(2)]
        at("rowG", [1, 512], BF16, 4.0)
        at("rowR2", [1, 512], BF16, 5.0)
        at("rowLn", [1, 512], F32, 6.0)
        at("x2_sb", [128, 16 * 1024], F32, OV)
        at("ln2_sb", [128, 16 * 1024], BF16, OV + 64)
        H["wbuf"] = [at(f"wbuf{i}", [128, 16384], BF16, OV + 96 + 32 * i)
                     for i in range(2)]
        at("hact_sb", [128, 8 * 1024], BF16, OV + 160)
        H["gtmp_sb"] = [at(f"gtmp{i}", [128, 512], BF16, OV + 176 + i)
                        for i in range(2)]
        at("r2bb16", [128, 1024], BF16, OV + 160)
        at("mr2bb16", [128, 1024], BF16, OV + 162)
        H["rsld2"] = [at("rsld_a", [128, 16 * 128], F32, OV + 172),
                      at("rsld_b", [128, 16 * 128], F32, OV + 152)]

        H["pp"] = [e(nc.psum_tensor(f"pp{i}", [128, 1024], F32))
                   for i in range(4)]

        sems = {name: e(nc.semaphore(name)) for name in PRODUCERS}
        block = e(nc.Block())

        def run(which):
            def body(eng):
                emit_all(Walk(nc, which, eng, sems), H)
            return body

        block.sync(run("sync"))
        block.tensor(run("tensor"))
        block.scalar(run("scalar"))
        block.vector(run("vector"))
        block.gpsimd(run("gpsimd"))

    return nc



def emit_all(W, H):
    pp = H["pp"]
    ones_c, ones_r = H["ones_c"], H["ones_r"]
    rowA, rowB = H["rowA"], H["rowB"]
    rowD, rowS, rowW, rowG = H["rowD"], H["rowS"], H["rowW"], H["rowG"]
    Q_sb, K_sb, V_sb, watt_sb = H["Q_sb"], H["K_sb"], H["V_sb"], H["watt_sb"]
    xc_sb, xq_sb = H["xc_sb"], H["xq_sb"]

    toks = []
    for dname, sname in (("bcol_sb", "bcol"), ("bvrow_f", "bvrow"),
                         ("gsc_sb", "gsc"), ("usc_sb", "usc"),
                         ("dsc_sb", "dsc"), ("bproj_sb", "bproj"),
                         ("ones_r", "onesrow"), ("ones_c", "onescol")):
        dst, src = H[dname], H[sname]
        full = tuple(slice(None) for _ in dst.shape)
        toks.append(W.do("sync", lambda e, d=dst, s=src, f=full: e.dma_start(
            out=d[f], in_=s), inc="qS", n=16))
    tok_const = toks[-1]
    tok_bvb = W.do("vector", lambda e: e.tensor_copy(
        H["bvrow_b"][:, :], H["bvrow_f"][:, :]), deps=[toks[-1]], inc="vV")
    tok_orb = W.do("gpsimd", lambda e: e.dma_start(
        out=H["ones_rb"][:, :], in_=H["onesrow"]), inc="qG", n=16)
    tok_watt = W.do("gpsimd", lambda e: e.dma_start(
        out=watt_sb[:, :].rearrange("p (c n) -> p c n", n=768),
        in_=H["watt"].rearrange("(c p) n -> p c n", p=128)), inc="qG", n=16)

    sx_ps = pp[0][0:1, 0:512]
    sq_ps = pp[0][0:1, 512:1024]
    rb_ps = pp[1][:, 0:512]
    mrb_ps = pp[1][:, 512:1024]
    qkv_ps = [pp[2][:, 0:512], pp[2][:, 512:1024]]
    v_ps = [pp[3][:, 0:256], pp[3][:, 512:768]]

    war_xc_h = [[[], []], [[], []]]
    war_xq = [[], []]
    war_sx = [tok_watt]
    war_rb = []
    war_rbb = []
    war_mrb = []
    war_mrbb = []
    war_qkv = [[], [], [], []]
    war_vps = [[], [], [], []]
    war_rowA = []
    war_rowB = []
    war_rowD = []
    last_p1_evac = None
    last_p1_mm = None

    def p1_prep(g):
        b = g % 2
        xcg, xqg = xc_sb[b], xq_sb[b]
        tok_ld = W.do("gpsimd", lambda e, g=g, xcg=xcg: e.dma_start(
            out=xcg[:, :].rearrange("p (c t) -> p c t", t=GQ),
            in_=H["xfeat"][:, g * GQ:(g + 1) * GQ].rearrange(
                "(c p) t -> p c t", p=128)),
            deps=[war_xc_h[b][0], war_xc_h[b][1]], inc="qG", n=16)
        tok_sx = None
        for hm in range(16):
            tok_sx = W.do("tensor", lambda e, hm=hm, xcg=xcg: e.matmul(
                sx_ps, ones_c[:, :], xcg[:, hm * GQ:(hm + 1) * GQ],
                start=hm == 0, stop=hm == 15),
                deps=[tok_ld, war_sx, tok_const] if hm == 0 else (),
                inc="tT" if hm == 15 else None)
        sq_toks = []
        for sb_i in range(4):
            tok = W.do("scalar", lambda e, sb_i=sb_i, xcg=xcg, xqg=xqg:
                       e.activation(xqg[:, sb_i * 2048:(sb_i + 1) * 2048],
                                    xcg[:, sb_i * 2048:(sb_i + 1) * 2048],
                                    AF.Square),
                       deps=[tok_ld, war_xq[b]] if sb_i == 0 else (),
                       inc="sS")
            sq_toks.append(tok)
        tok_sq = None
        for hm in range(16):
            tok_sq = W.do("tensor", lambda e, hm=hm, xqg=xqg: e.matmul(
                sq_ps, ones_c[:, :], xqg[:, hm * GQ:(hm + 1) * GQ],
                start=hm == 0, stop=hm == 15),
                deps=[sq_toks[hm // 4]] if hm % 4 == 0 else (),
                inc="tT" if hm == 15 else None)
        W.do("scalar", lambda e: e.mul(rowA[:, 0:512], sx_ps, 1.0 / 2048.0),
             deps=[tok_sx, war_rowA])
        tok_st = W.do("scalar", lambda e: e.mul(
            rowB[:, 0:512], sq_ps, 1.0 / 2048.0),
            deps=[tok_sq, war_rowB], inc="sS")
        return tok_st, tok_sx, tok_sq

    def p1_statsvec(tok_st, tok_rb_prev):
        W.do("vector", lambda e: e.tensor_mul(
            rowS[:, 0:512], rowA[:, 0:512], rowA[:, 0:512]),
            deps=[tok_st, tok_rb_prev])
        W.do("vector", lambda e: e.scalar_tensor_tensor(
            rowB[:, 0:512], rowB[:, 0:512], EPS, rowS[:, 0:512],
            op0=ALU.add, op1=ALU.subtract))
        tok_vv = W.do("vector", lambda e: e.tensor_copy(
            rowB[:, 0:512], rowB[:, 0:512]), inc="vV")
        W.do("scalar", lambda e: e.activation(
            rowB[:, 0:512], rowB[:, 0:512], AF.Ln), deps=[tok_vv])
        tok_r = W.do("scalar", lambda e: e.activation(
            rowS[:, 0:512], rowB[:, 0:512], AF.Exp, scale=-0.5), inc="sS")
        tok_mr = W.do("vector", lambda e: e.tensor_mul(
            rowD[:, 0:512], rowA[:, 0:512], rowS[:, 0:512]),
            deps=[tok_r, war_rowD], inc="vV")
        return tok_r, tok_mr

    st0, sx0, sq0 = p1_prep(0)
    prep = {"st": st0, "sx": sx0, "sq": sq0}
    tok_r, tok_mr = p1_statsvec(st0, None)

    for g in range(NG):
        b = g % 2
        xcg, xqg = xc_sb[b], xq_sb[b]
        cur_st, cur_sx, cur_sq = prep["st"], prep["sx"], prep["sq"]
        cur_r, cur_mr = tok_r, tok_mr
        war_sx = [cur_st]
        war_rowA = [cur_mr]
        war_rowB = [cur_r]
        tok_rb = W.do("tensor", lambda e: e.matmul(
            rb_ps, ones_r[:, :], rowS[0:1, 0:512], start=True, stop=True),
            deps=[cur_r, war_rb, tok_const], inc="tT")
        tok_mb = W.do("tensor", lambda e: e.matmul(
            mrb_ps, ones_r[:, :], rowD[0:1, 0:512], start=True, stop=True),
            deps=[cur_mr, war_mrb], inc="tT")
        war_rowD = [tok_mb]
        tok_rbc = W.do("vector", lambda e: e.tensor_copy(
            H["rbb16"][:, :], rb_ps), deps=[tok_rb, war_rbb], inc="vV")
        tok_mbc = W.do("vector", lambda e: e.tensor_copy(
            H["mrb16"][:, :], mrb_ps), deps=[tok_mb, war_mrbb], inc="vV")
        tt_toks = []
        for hm in range(16):
            W.do("vector", lambda e, hm=hm, xcg=xcg, xqg=xqg: e.tensor_tensor(
                xqg[:, hm * GQ:(hm + 1) * GQ],
                xcg[:, hm * GQ:(hm + 1) * GQ],
                H["rbb16"][:, :], op=ALU.mult),
                deps=[cur_sq] if hm == 0 else ())
            tok = W.do("vector", lambda e, hm=hm, xqg=xqg: e.tensor_tensor(
                xqg[:, hm * GQ:(hm + 1) * GQ],
                xqg[:, hm * GQ:(hm + 1) * GQ],
                H["mrb16"][:, :], op=ALU.subtract), inc="vV")
            tt_toks.append(tok)
        if g + 1 < NG:
            nst, nsx, nsq = p1_prep(g + 1)
            prep = {"st": nst, "sx": nsx, "sq": nsq}
            tok_r, tok_mr = p1_statsvec(prep["st"], tok_rb)
        qk_slots = [pp[2][:, 0:512], pp[2][:, 512:1024],
                    rb_ps, mrb_ps]
        v_slots = [pp[3][:, 0:256], pp[3][:, 512:768]]
        qk_last = [None] * 4
        v_last = [None] * 4
        for hm in range(16):
            for cc in range(4):
                qk_last[cc] = W.do("tensor", lambda e, hm=hm, cc=cc,
                                   xqg=xqg: e.matmul(
                    qk_slots[cc], watt_sb[:, hm * 768 + cc * 128:
                                          hm * 768 + (cc + 1) * 128],
                    xqg[:, hm * GQ:(hm + 1) * GQ],
                    start=hm == 0, stop=hm == 15),
                    deps=[tt_toks[hm], war_qkv[cc],
                          tok_rbc if cc == 2 else None,
                          tok_mbc if cc == 3 else None],
                    inc="tT" if hm == 15 else None)
            for tt in range(2):
                v_last[tt] = W.do("tensor", lambda e, hm=hm, tt=tt,
                                  xqg=xqg: e.matmul(
                    v_slots[tt], xqg[:, hm * GQ + tt * 128:
                                     hm * GQ + (tt + 1) * 128],
                    watt_sb[:, hm * 768 + 512: hm * 768 + 768],
                    start=hm == 0, stop=hm == 15),
                    deps=[tt_toks[hm], war_vps[tt]],
                    inc="tT" if hm == 15 else None)
        for tt in range(2):
            kt = 4 * g + tt
            ev = W.do("scalar", lambda e, kt=kt, tt=tt: e.activation(
                V_sb[:, kt * 256:(kt + 1) * 256], v_slots[tt], AF.Identity),
                deps=[v_last[tt]], inc="sS")
            war_vps[tt] = [ev]
        for hm in range(16):
            for tt in range(2, 4):
                v_last[tt] = W.do("tensor", lambda e, hm=hm, tt=tt,
                                  xqg=xqg: e.matmul(
                    v_slots[tt - 2], xqg[:, hm * GQ + tt * 128:
                                         hm * GQ + (tt + 1) * 128],
                    watt_sb[:, hm * 768 + 512: hm * 768 + 768],
                    start=hm == 0, stop=hm == 15),
                    deps=[war_vps[tt - 2], war_vps[tt]] if hm == 0 else (),
                    inc="tT" if hm == 15 else None)
        for cc in range(4):
            dst = Q_sb if cc < 2 else K_sb
            hh = cc % 2
            ev = W.do("scalar", lambda e, dst=dst, hh=hh, g=g, cc=cc:
                      e.activation(dst[:, hh * T + g * GQ:
                                       hh * T + (g + 1) * GQ],
                                   qk_slots[cc], AF.Identity,
                                   bias=H["bcol_sb"][:, cc:cc + 1]),
                      deps=[qk_last[cc]], inc="sS")
            war_qkv[cc] = [ev]
        war_rb = [war_qkv[2]]
        war_mrb = [war_qkv[3]]
        for tt in range(2, 4):
            kt = 4 * g + tt
            ev = W.do("scalar", lambda e, kt=kt, tt=tt: e.activation(
                V_sb[:, kt * 256:(kt + 1) * 256], v_slots[tt - 2],
                AF.Identity),
                deps=[v_last[tt]], inc="sS")
            war_vps[tt] = [ev]
            last_p1_evac = ev
        last_p1_mm = v_last[3]
        war_xq[b] = [v_last[3], tt_toks[15]]
        war_rbb = [tt_toks[15]]
        war_mrbb = [tt_toks[15]]
        war_xc_h[b][0] = [tt_toks[7], cur_sx]
        war_xc_h[b][1] = [tt_toks[15], cur_sx]

    ph1 = [last_p1_evac, last_p1_mm]

    sqs_sb = H["sqs_sb"]
    qn_ps = pp[1][0:1, 512:1024]
    negm_toks = []
    war_sqs = [[ph1], [ph1]]
    war_qn = []
    tok_bc = None
    for h in range(2):
        for si, src in enumerate((Q_sb, K_sb)):
            run_off = h * 2 + si
            for ck in range(16):
                sb = sqs_sb[ck % 2]
                tok_q2 = W.do("scalar", lambda e, src=src, h=h, ck=ck, sb=sb:
                              e.activation(sb[:, :],
                                           src[:, h * T + ck * 512:
                                               h * T + (ck + 1) * 512],
                                           AF.Square),
                              deps=[ph1, war_sqs[ck % 2]], inc="sS")
                tok_nm = W.do("tensor", lambda e, sb=sb: e.matmul(
                    qn_ps, ones_c[:, :], sb[:, :], start=True, stop=True),
                    deps=[tok_q2, war_qn, war_mrb, war_mrbb], inc="tT")
                war_sqs[ck % 2] = [tok_nm]
                if ck == 0:
                    tok_red = W.do("vector", lambda e, run_off=run_off:
                                   e.reduce_max(rowS[0:1, run_off:run_off + 1],
                                                qn_ps, axis=AX.X),
                                   deps=[tok_nm], inc="vV")
                else:
                    W.do("vector", lambda e: e.reduce_max(
                        rowS[0:1, 8:9], qn_ps, axis=AX.X), deps=[tok_nm])
                    tok_red = W.do("vector", lambda e, run_off=run_off:
                                   e.tensor_max(rowS[0:1, run_off:run_off + 1],
                                                rowS[0:1, run_off:run_off + 1],
                                                rowS[0:1, 8:9]), inc="vV")
                war_qn = [tok_red]
        tok_p = W.do("vector", lambda e, h=h: e.tensor_mul(
            rowS[0:1, 9:10], rowS[0:1, 2 * h:2 * h + 1],
            rowS[0:1, 2 * h + 1:2 * h + 2]), inc="vV")
        W.do("scalar", lambda e: e.activation(
            rowS[0:1, 9:10], rowS[0:1, 9:10], AF.Sqrt, scale=1.0 / 128.0),
            deps=[tok_p])
        tok_ng = W.do("scalar", lambda e: e.mul(
            rowS[0:1, 10:11], rowS[0:1, 9:10], -1.0), inc="sS")
        tok_bc = W.do("tensor", lambda e: e.matmul(
            pp[0][:, 0:1], ones_r[:, :], rowS[0:1, 10:11],
            start=True, stop=True), deps=[tok_ng, war_sx], inc="tT")
        tok_nm2 = W.do("vector", lambda e, h=h: e.tensor_copy(
            H["negm_sb"][:, h:h + 1], pp[0][:, 0:1]), deps=[tok_bc], inc="vV")
        negm_toks.append(tok_nm2)

    o_sb, prst_sb, p_sb = H["o_sb"], H["prst_sb"], H["p_sb"]
    wp_sb, maskp_sb, rbs_sb = H["wp_sb"], H["maskp_sb"], H["rbs_sb"]
    rs_in, rs_out = H["rs_in"], H["rs_out"]

    tok_wp = W.do("gpsimd", lambda e: e.dma_start(
        out=wp_sb[:, :].rearrange("p (c n) -> p c n", n=2048),
        in_=H["wpj"].rearrange("(c p) n -> p c n", p=128)),
        deps=[ph1], inc="qG", n=16)
    tok_mk = W.do("sync", lambda e: e.dma_start(
        out=maskp_sb[:, :], in_=H["maskp"]), deps=[ph1], inc="qS", n=16)

    sums_ps = pp[3][0:1, 0:512]
    rb2_ps = pp[3][:, 512:1024]
    o_ps = [pp[2][:, 0:512], pp[2][:, 512:1024]]

    war_sc = [[negm_toks[1], tok_bc], [war_qn]]
    war_p = [[], []]
    war_o = [war_qkv[0], war_qkv[1]]
    war_sums = [war_vps[0], war_vps[1]]
    war_rb2 = [war_vps[2], war_vps[3]]
    war_prst = []
    war_rowE2 = [tok_bc]
    war_rowG = []
    war_rc2 = []
    war_rbs = []
    pair_idx = 0
    proj_slot_war = [[], [], [], []]
    qc_count = 0

    def emit_tail(h, g, ob, tok_sum_last):
        nonlocal war_sums, war_rowG, war_rc2, war_rb2, war_rbs
        tok_rg = W.do("scalar", lambda e: e.activation(
            rowG[0:1, 0:512], sums_ps, AF.Copy),
            deps=[tok_sum_last, war_rowG], inc="sS")
        W.do("scalar", lambda e: e.activation(
            H["rowLn"][0:1, 0:512], sums_ps, AF.Ln),
            deps=[tok_sum_last])
        tok_rc = W.do("scalar", lambda e: e.activation(
            H["rowR2"][0:1, 0:512], H["rowLn"][0:1, 0:512], AF.Exp,
            scale=-1.0), deps=[war_rc2], inc="sS")
        war_sums = [tok_rg, tok_rc]
        tok_fk = W.do("tensor", lambda e, h=h, ob=ob: e.matmul(
            o_ps[ob], H["bvrow_b"][0:1, h * 128:(h + 1) * 128],
            rowG[0:1, 0:512], start=False, stop=True),
            deps=[tok_rg, tok_bvb], inc="tT")
        war_rowG = [tok_fk]
        tok_b2 = W.do("tensor", lambda e: e.matmul(
            rb2_ps, H["ones_rb"][:, :], H["rowR2"][0:1, 0:512],
            start=True, stop=True),
            deps=[tok_rc, war_rb2, tok_orb], inc="tT")
        war_rc2 = [tok_b2]
        tok_cp = W.do("scalar", lambda e: e.activation(
            rbs_sb[:, :], rb2_ps, AF.Copy), deps=[tok_b2, war_rbs], inc="sS")
        war_rb2 = [tok_cp]
        tok_oe = W.do("vector", lambda e, h=h, g=g, ob=ob: e.tensor_tensor(
            o_sb[:, h * T + g * GQ:h * T + (g + 1) * GQ], o_ps[ob],
            rbs_sb[:, :], op=ALU.mult), deps=[tok_fk, tok_cp], inc="vV")
        war_rbs = [tok_oe]
        war_o[ob] = [tok_oe]
        return tok_oe, tok_fk

    tok_fk = tok_pm = tok_pe = tok_oe = None
    for g in range(16):
        oe_toks = []
        tail0 = None
        for h in range(2):
            npairs = 2 * g + 2
            ob = (2 * g + h) % 2
            tok_sum_last = None
            pend = None
            for pi in range(npairs + 1):
                if pi < npairs:
                    pb = pair_idx % 2
                    pair_idx += 1
                    sc = pp[pb]
                    kt0, kt1 = 2 * pi, 2 * pi + 1
                    W.do("tensor", lambda e, h=h, g=g, kt0=kt0, sc=sc:
                         e.matmul(
                        sc[:, 0:512],
                        K_sb[:, h * T + kt0 * 128:h * T + (kt0 + 1) * 128],
                        Q_sb[:, h * T + g * GQ:h * T + (g + 1) * GQ],
                        start=True, stop=True),
                        deps=[war_sc[pb], ph1])
                    tok_s2 = W.do("tensor", lambda e, h=h, g=g, kt1=kt1, sc=sc:
                                  e.matmul(
                        sc[:, 512:1024],
                        K_sb[:, h * T + kt1 * 128:h * T + (kt1 + 1) * 128],
                        Q_sb[:, h * T + g * GQ:h * T + (g + 1) * GQ],
                        start=True, stop=True), inc="tT")
                    tok_exp = W.do("scalar", lambda e, h=h, pb=pb, sc=sc:
                                   e.activation(p_sb[pb][:, :], sc[:, :],
                                                AF.Exp,
                                                bias=H["negm_sb"][:, h:h + 1],
                                                scale=SCALE),
                                   deps=[tok_s2, war_p[pb], negm_toks[1]],
                                   inc="sS")
                    war_sc[pb] = [tok_exp]
                    last_exp = tok_exp
                    gate_tok = tok_exp
                    if pi >= 2 * g:
                        rel = pi - 2 * g
                        gate_tok = W.do("vector", lambda e, pb=pb, rel=rel:
                                        e.tensor_tensor(
                                            p_sb[pb][:, :], p_sb[pb][:, :],
                                            maskp_sb[:, rel * 1024:
                                                     (rel + 1) * 1024],
                                            op=ALU.mult),
                                        deps=[tok_exp, tok_mk], inc="vV")
                    cur = (pb, gate_tok, kt0, kt1, pi)
                else:
                    cur = None
                if tail0 is not None and h == 1 and pi == 1:
                    oe0, _ = emit_tail(0, g, tail0[0], tail0[1])
                    oe_toks.append(oe0)
                    tail0 = None
                if pend is not None:
                    pb, gate_tok, kt0, kt1, ppi = pend
                    W.do("tensor", lambda e, pb=pb, kt0=kt0, h=h, ob=ob:
                         e.matmul(
                        o_ps[ob],
                        V_sb[:, kt0 * 256 + h * 128:kt0 * 256 + (h + 1) * 128],
                        p_sb[pb][:, 0:512], start=ppi == 0, stop=False),
                        deps=[gate_tok, war_o[ob]] if ppi == 0
                        else [gate_tok])
                    tok_av = W.do("tensor", lambda e, pb=pb, kt1=kt1, h=h,
                                  ob=ob: e.matmul(
                        o_ps[ob],
                        V_sb[:, kt1 * 256 + h * 128:kt1 * 256 + (h + 1) * 128],
                        p_sb[pb][:, 512:1024], start=False, stop=False))
                    W.do("tensor", lambda e, pb=pb: e.matmul(
                        sums_ps, ones_c[:, :], p_sb[pb][:, 0:512],
                        start=ppi == 0, stop=False),
                        deps=[war_sums] if ppi == 0 else ())
                    tok_sum_last = W.do("tensor", lambda e, pb=pb: e.matmul(
                        sums_ps, ones_c[:, :], p_sb[pb][:, 512:1024],
                        start=False, stop=ppi == npairs - 1), inc="tT")
                    war_p[pb] = [tok_sum_last]
                pend = cur
            if h == 0:
                tail0 = (ob, tok_sum_last)
                if g == 0:
                    oe0, _ = emit_tail(0, g, ob, tok_sum_last)
                    oe_toks.append(oe0)
                    tail0 = None
            else:
                tok_rg1 = W.do("scalar", lambda e: e.activation(
                    rowG[0:1, 0:512], sums_ps, AF.Copy),
                    deps=[tok_sum_last, war_rowG], inc="sS")
                W.do("scalar", lambda e: e.activation(
                    H["rowLn"][0:1, 0:512], sums_ps, AF.Ln),
                    deps=[tok_sum_last])
                tok_rc1 = W.do("scalar", lambda e: e.activation(
                    H["rowR2"][0:1, 0:512], H["rowLn"][0:1, 0:512], AF.Exp,
                    scale=-1.0), deps=[war_rc2], inc="sS")
                war_sums = [tok_rg1, tok_rc1]
                for cc in range(4):
                    slot = cc % 4
                    ps = pp[slot // 2][:, (slot % 2) * 512:(slot % 2 + 1) * 512]
                    W.do("tensor", lambda e, cc=cc, g=g, ps=ps: e.matmul(
                        ps, wp_sb[:, cc * 128:(cc + 1) * 128],
                        o_sb[:, g * GQ:(g + 1) * GQ], start=True, stop=False),
                        deps=[oe_toks[0], proj_slot_war[slot], tok_wp,
                              war_sc[slot // 2]])
                tok_fk1 = W.do("tensor", lambda e, h=h, ob=ob: e.matmul(
                    o_ps[ob], H["bvrow_b"][0:1, h * 128:(h + 1) * 128],
                    rowG[0:1, 0:512], start=False, stop=True),
                    deps=[tok_rg1, tok_bvb], inc="tT")
                war_rowG = [tok_fk1]
                tok_b21 = W.do("tensor", lambda e: e.matmul(
                    rb2_ps, H["ones_rb"][:, :], H["rowR2"][0:1, 0:512],
                    start=True, stop=True),
                    deps=[tok_rc1, war_rb2, tok_orb], inc="tT")
                war_rc2 = [tok_b21]
                tok_cp1 = W.do("scalar", lambda e: e.activation(
                    rbs_sb[:, :], rb2_ps, AF.Copy),
                    deps=[tok_b21, war_rbs], inc="sS")
                war_rb2 = [tok_cp1]
                tok_oe = W.do("vector", lambda e, h=h, g=g, ob=ob:
                              e.tensor_tensor(
                    o_sb[:, h * T + g * GQ:h * T + (g + 1) * GQ], o_ps[ob],
                    rbs_sb[:, :], op=ALU.mult), deps=[tok_fk1, tok_cp1],
                    inc="vV")
                war_rbs = [tok_oe]
                war_o[ob] = [tok_oe]
                oe_toks.append(tok_oe)
                tok_fk = tok_fk1
        half_toks = [[], []]
        for cc in range(16):
            slot = cc % 4
            ps = pp[slot // 2][:, (slot % 2) * 512:(slot % 2 + 1) * 512]
            if cc >= 4:
                W.do("tensor", lambda e, cc=cc, g=g, ps=ps: e.matmul(
                    ps, wp_sb[:, cc * 128:(cc + 1) * 128],
                    o_sb[:, g * GQ:(g + 1) * GQ], start=True, stop=False),
                    deps=[oe_toks[0], proj_slot_war[slot], tok_wp,
                          war_sc[slot // 2]])
            tok_pm = W.do("tensor", lambda e, cc=cc, g=g, ps=ps: e.matmul(
                ps, wp_sb[:, 2048 + cc * 128:2048 + (cc + 1) * 128],
                o_sb[:, T + g * GQ:T + (g + 1) * GQ],
                start=False, stop=True), deps=[oe_toks[1]], inc="tT")
            tok_pe = W.do("vector", lambda e, cc=cc, ps=ps: e.tensor_scalar(
                prst_sb[:, :].rearrange("p (tt c t) -> p tt c t",
                                        tt=4, c=16)[:, :, cc, :],
                ps.rearrange("p (a b) -> p a b", b=128),
                H["bproj_sb"][:, cc:cc + 1], None, op0=ALU.add),
                deps=[tok_pm, war_prst, tok_const], inc="vV")
            proj_slot_war[slot] = [tok_pe]
            half_toks[0].append(tok_pe)
        war_sc[0] = [war_sc[0], proj_slot_war[0], proj_slot_war[1]]
        war_sc[1] = [war_sc[1], proj_slot_war[2], proj_slot_war[3]]
        j, half = g // 2, g % 2
        tok_dma = W.do("scalar", lambda e, j=j, half=half: e.dma_start(
            out=rs_in[j].ap()[half * 4:(half + 1) * 4].rearrange(
                "r (c p2) t -> p2 (r c) t", p2=128),
            in_=prst_sb[:, :].rearrange("p (x t) -> p x t", t=128)),
            deps=[half_toks[0], war_prst], inc="qC", n=16)
        war_prst = [tok_dma]
        qc_count = tok_dma[1]
        if half == 1 and j < 7:
            W.do("gpsimd", lambda e, j=j: e.collective_compute(
                "ReduceScatter", ALU.add,
                replica_groups=[list(range(NC))],
                ins=[rs_in[j].ap().opt()],
                outs=[rs_out[j].ap().opt()]),
                deps=[("qC", qc_count)], inc="cc")

    ph2_t = [tok_pm, tok_fk]
    ph2_v = [tok_pe, tok_oe, war_rb2, war_sums]
    ph2 = [ph2_t, ph2_v, ("qC", qc_count), last_exp]
    ph2_nc = [ph2_t, ph2_v, last_exp]

    x2_sb, ln2_sb, hact_sb = H["x2_sb"], H["ln2_sb"], H["hact_sb"]
    wbuf, gtmp_sb = H["wbuf"], H["gtmp_sb"]

    xo_toks = []
    for j in range(8):
        tok = W.do("sync", lambda e, j=j: e.dma_start(
            out=x2_sb[:, :].rearrange("p (c t) -> p c t", t=1024)[
                :, :, j * 128:(j + 1) * 128],
            in_=H["xown"].rearrange("(c p) t -> p c t", p=128)[
                :, :, j * 128:(j + 1) * 128]),
            deps=[ph2_nc] if j == 0 else [xo_toks[j - 1]], inc="qS", n=16)
        xo_toks.append(tok)

    sx3_ps = pp[0][0:1, :]
    sq3_ps = pp[1][0:1, :]
    r2b_ps = pp[3]
    mr2b_ps = pp[2]
    sq_scr = hact_sb[:, 2048:4096]

    war_rsld2 = [[], []]
    war_pp0 = [ph2_nc]
    war_sqscr = []
    war_rowS3 = [war_rowE2]
    tok_sx3 = tok_sq3 = tok_q3 = None
    tok_rl6 = None
    tok_rl_prev = None
    tok_ln2 = None
    last_mul = None
    prev_cpR = prev_cpM = None
    for j in range(8):
        js = slice(j * 128, (j + 1) * 128)
        rsld_b = H["rsld2"][j % 2]
        rl_eng, rl_q = ("gpsimd", "qG") if j < 7 else ("scalar", "qC")
        deps_rl = [("cc", j + 1), war_rsld2[j % 2], tok_rl_prev]
        if j < 2:
            deps_rl += [ph2_v, last_exp]
        if j == 1:
            deps_rl += [("qC", qc_count)]
        tok_rl = W.do(rl_eng, lambda e, j=j, rb=rsld_b: e.dma_start(
            out=rb[:, :].rearrange("p (c t) -> p c t", t=128),
            in_=rs_out[j].ap().rearrange("(c p2) t -> p2 c t", p2=128)),
            deps=deps_rl, inc=rl_q, n=16)
        tok_rl_prev = tok_rl
        if j == 6:
            tok_rl6 = tok_rl
        if j == 1:
            W.do("gpsimd", lambda e: e.collective_compute(
                "ReduceScatter", ALU.add,
                replica_groups=[list(range(NC))],
                ins=[rs_in[7].ap().opt()],
                outs=[rs_out[7].ap().opt()]),
                deps=[("qC", qc_count)], inc="cc")
        tok_stt = W.do("vector", lambda e, j=j, rb=rsld_b: e.tensor_tensor(
            x2_sb[:, :].rearrange("p (c t) -> p c t", t=1024)[
                :, :, j * 128:(j + 1) * 128],
            x2_sb[:, :].rearrange("p (c t) -> p c t", t=1024)[
                :, :, j * 128:(j + 1) * 128],
            rb[:, :].rearrange("p (c t) -> p c t", t=128),
            op=ALU.add),
            deps=[tok_rl, xo_toks[j], ph2_v], inc="vV")
        war_rsld2[j % 2] = [tok_stt]
        tok_st1 = W.do("vector", lambda e, j=j: e.tensor_copy(
            ln2_sb[:, :].rearrange("p (c t) -> p c t", t=1024)[
                :, :, j * 128:(j + 1) * 128],
            x2_sb[:, :].rearrange("p (c t) -> p c t", t=1024)[
                :, :, j * 128:(j + 1) * 128]),
            deps=[ph2_t], inc="vV")
        for cc in range(16):
            tok_sx3 = W.do("tensor", lambda e, j=j, cc=cc: e.matmul(
                sx3_ps[:, js], ones_c[:, :],
                ln2_sb[:, cc * 1024 + j * 128:cc * 1024 + (j + 1) * 128],
                start=cc == 0, stop=cc == 15),
                deps=[tok_st1, war_pp0] if cc == 0 else (),
                inc="tT" if cc == 15 else None)
        tok_q3 = W.do("scalar", lambda e, j=j: e.activation(
            sq_scr[:, 0:2048].rearrange("p (c t) -> p c t", t=128),
            x2_sb[:, :].rearrange("p (c t) -> p c t", t=1024)[
                :, :, j * 128:(j + 1) * 128],
            AF.Square), deps=[tok_stt, ph2_nc, war_sqscr], inc="sS")
        for cc in range(16):
            tok_sq3 = W.do("tensor", lambda e, j=j, cc=cc: e.matmul(
                sq3_ps[:, js], ones_c[:, :],
                sq_scr[:, cc * 128:(cc + 1) * 128],
                start=cc == 0, stop=cc == 15),
                deps=[tok_q3, ph2_nc] if cc == 0 else (),
                inc="tT" if cc == 15 else None)
        war_sqscr = [tok_sq3]
        W.do("scalar", lambda e, js=js: e.mul(
            rowA[0:1, js], sx3_ps[0:1, js], 1.0 / 2048.0), deps=[tok_sx3])
        last_mul = W.do("scalar", lambda e, js=js: e.mul(
            rowB[0:1, js], sq3_ps[0:1, js], 1.0 / 2048.0),
            deps=[tok_sq3], inc="sS")
        war_pp0 = [last_mul]
        W.do("vector", lambda e, js=js: e.tensor_mul(
            rowS[0:1, js], rowA[0:1, js], rowA[0:1, js]),
            deps=[last_mul, war_rowS3])
        war_rowS3 = []
        tok_v3 = W.do("vector", lambda e, js=js: e.scalar_tensor_tensor(
            rowB[0:1, js], rowB[0:1, js], EPS, rowS[0:1, js],
            op0=ALU.add, op1=ALU.subtract), inc="vV")
        W.do("scalar", lambda e, js=js: e.activation(
            rowB[0:1, js], rowB[0:1, js], AF.Ln), deps=[tok_v3])
        tok_r3 = W.do("scalar", lambda e, js=js: e.activation(
            rowS[0:1, js], rowB[0:1, js], AF.Exp, scale=-0.5), inc="sS")
        tok_mr3 = W.do("vector", lambda e, js=js: e.tensor_mul(
            rowD[0:1, js], rowA[0:1, js], rowS[0:1, js]),
            deps=[tok_r3], inc="vV")
        tok_b3 = W.do("tensor", lambda e, js=js: e.matmul(
            r2b_ps[:, js], ones_r[:, :], rowS[0:1, js],
            start=True, stop=True),
            deps=([tok_r3, ph2_nc, war_sums, war_rb2] if j == 0
                  else [tok_r3, prev_cpR]), inc="tT")
        tok_m3 = W.do("tensor", lambda e, js=js: e.matmul(
            mr2b_ps[:, js], ones_r[:, :], rowD[0:1, js],
            start=True, stop=True),
            deps=([tok_mr3, ph2_nc] if j == 0 else [tok_mr3, prev_cpM]),
            inc="tT")
        prev_cpR = W.do("scalar", lambda e, js=js: e.activation(
            H["r2bb16"][:, js], r2b_ps[:, js], AF.Copy),
            deps=[tok_b3], inc="sS")
        prev_cpM = W.do("scalar", lambda e, js=js: e.activation(
            H["mr2bb16"][:, js], mr2b_ps[:, js], AF.Copy),
            deps=[tok_m3], inc="sS")
        W.do("vector", lambda e, j=j, js=js: e.tensor_tensor(
            ln2_sb[:, :].rearrange("p (c t) -> p c t", t=1024)[
                :, :, j * 128:(j + 1) * 128],
            x2_sb[:, :].rearrange("p (c t) -> p c t", t=1024)[
                :, :, j * 128:(j + 1) * 128],
            H["r2bb16"][:, js].unsqueeze(1).broadcast_to([128, 16, 128]),
            op=ALU.mult), deps=[prev_cpR, prev_cpM, tok_sx3])
        tok_ln2 = W.do("vector", lambda e, j=j, js=js: e.tensor_tensor(
            ln2_sb[:, :].rearrange("p (c t) -> p c t", t=1024)[
                :, :, j * 128:(j + 1) * 128],
            ln2_sb[:, :].rearrange("p (c t) -> p c t", t=1024)[
                :, :, j * 128:(j + 1) * 128],
            H["mr2bb16"][:, js].unsqueeze(1).broadcast_to([128, 16, 128]),
            op=ALU.subtract), inc="vV")
        if j == 3:
            tok_ln2_h0 = tok_ln2

    mm_slots = [pp[0][:, 0:512], pp[0][:, 512:1024],
                pp[1][:, 0:512], pp[1][:, 512:1024]]
    war_wbuf = [[ph2], [ph2, war_rsld2[1], tok_rl_prev]]
    war_slot = [[last_mul], [last_mul], [last_mul], [last_mul]]
    war_hact = [ph2, tok_sq3]
    war_gtmp = [[ph2], [ph2]]
    war_x2 = [tok_ln2, tok_q3]
    prev_wd = None
    sidx = 0
    tok_out_ready = None
    for p in range(NPASS):
        htoks = {}
        for stage in ("up", "gate", "down"):
            wb = wbuf[sidx % 2]
            wpar = sidx % 2
            sidx += 1
            if stage == "gate":
                src = H["gateT"]
            elif stage == "up":
                src = H["upT"]
            else:
                src = H["downT"]
            if stage in ("gate", "up"):
                dma_in = src[:, p * ISPAN:(p + 1) * ISPAN].rearrange(
                    "(c p2) n -> p2 c n", p2=128)
                dma_out = wb[:, :].rearrange("p (c n) -> p c n", c=16)
                nh = 512
            else:
                dma_in = src[p * ISPAN:(p + 1) * ISPAN, :].rearrange(
                    "(c p2) n -> p2 c n", p2=128)
                dma_out = wb[:, :].rearrange("p (c n) -> p c n", c=8)
                nh = 1024
            if True:
                tok_wh = [None, None]
                for hf in range(2):
                    tok_wd = W.do("gpsimd", lambda e, o=dma_out, i=dma_in,
                                  hf=hf, nh=nh: e.dma_start(
                        out=o[:, :, hf * nh:(hf + 1) * nh],
                        in_=i[:, :, hf * nh:(hf + 1) * nh]),
                        deps=[war_wbuf[wpar], tok_rl6, prev_wd]
                        if hf == 0 else [prev_wd],
                        inc="qG", n=16)
                    prev_wd = tok_wd
                    tok_wh[hf] = W.do("vector", lambda e, o=dma_out, hf=hf,
                                      nh=nh: e.tensor_scalar_add(
                        o[:, :, hf * nh:(hf + 1) * nh],
                        o[:, :, hf * nh:(hf + 1) * nh], -128.0),
                        deps=[tok_wd], inc="vV")
                tok_w = tok_wh
            if stage in ("gate", "up"):
                last_mm = None
                split_tg = (p == 0 and stage == "up")
                for ic8 in range(16 if split_tg else 8):
                    tgs = [ic8 // 8] if split_tg else [0, 1]
                    ic = ic8 % 8
                    gic = p * 8 + ic
                    ps2 = [mm_slots[(ic % 2) * 2], mm_slots[(ic % 2) * 2 + 1]]
                    tok_hm15 = [None, None]
                    for hm in range(16):
                        for tg in tgs:
                            tok_hm15[tg] = W.do("tensor", lambda e, hm=hm,
                                                ic=ic, tg=tg,
                                 wb=wb, ps2=ps2: e.matmul(
                                ps2[tg], wb[:, hm * 1024 + ic * 128:
                                            hm * 1024 + (ic + 1) * 128],
                                ln2_sb[:, hm * 1024 + tg * 512:
                                       hm * 1024 + (tg + 1) * 512],
                                start=hm == 0, stop=hm == 15),
                                deps=[tok_w[ic // 4],
                                      war_slot[(ic % 2) * 2 + tg],
                                      tok_ln2_h0 if (split_tg and tg == 0)
                                      else tok_ln2]
                                if hm == 0 else (),
                                inc="tT" if hm == 15 else None)
                    for tg in tgs:
                        slot = (ic % 2) * 2 + tg
                        ps = ps2[tg]
                        tok_mm = tok_hm15[tg]
                        last_mm = tok_mm
                        if stage == "up":
                            ev = W.do("scalar", lambda e, ic=ic, tg=tg,
                                      gic=gic, ps=ps: e.activation(
                                hact_sb[:, ic * 1024 + tg * 512:
                                        ic * 1024 + (tg + 1) * 512],
                                ps, AF.Identity,
                                scale=H["usc_sb"][:, gic:gic + 1]),
                                deps=[tok_mm, war_hact], inc="sS")
                            htoks[(ic, tg)] = ev
                            war_slot[slot] = [ev]
                        else:
                            gp = gtmp_sb[(ic * 2 + tg) % 2]
                            ev = W.do("scalar", lambda e, gp=gp, gic=gic,
                                      ps=ps: e.activation(
                                gp[:, :], ps, AF.Silu,
                                scale=H["gsc_sb"][:, gic:gic + 1]),
                                deps=[tok_mm,
                                      war_gtmp[(ic * 2 + tg) % 2]],
                                inc="sS")
                            war_slot[slot] = [ev]
                            tok_h = W.do("vector", lambda e, ic=ic, tg=tg,
                                         gp=gp: e.tensor_tensor(
                                hact_sb[:, ic * 1024 + tg * 512:
                                        ic * 1024 + (tg + 1) * 512],
                                hact_sb[:, ic * 1024 + tg * 512:
                                        ic * 1024 + (tg + 1) * 512],
                                gp[:, :], op=ALU.mult),
                                deps=[ev, htoks[(ic, tg)]], inc="vV")
                            war_gtmp[(ic * 2 + tg) % 2] = [tok_h]
                            htoks[(ic, tg)] = tok_h
                war_wbuf[wpar] = [last_mm]
            else:
                gate_done = [htoks[(ic, tg)]
                             for ic in range(8) for tg in range(2)]
                last_mm = None
                for cc in range(16):
                    ps2 = [mm_slots[(cc % 2) * 2], mm_slots[(cc % 2) * 2 + 1]]
                    tok_mms = [None, None]
                    for ic in range(8):
                        for tg in range(2):
                            tok_mms[tg] = W.do("tensor", lambda e, ic=ic,
                                               cc=cc, tg=tg, wb=wb, ps2=ps2:
                                               e.matmul(
                                ps2[tg], wb[:, ic * 2048 + cc * 128:
                                            ic * 2048 + (cc + 1) * 128],
                                hact_sb[:, ic * 1024 + tg * 512:
                                        ic * 1024 + (tg + 1) * 512],
                                start=ic == 0, stop=ic == 7),
                                deps=([tok_w[cc // 8],
                                       war_slot[(cc % 2) * 2 + tg]]
                                      + gate_done)
                                if ic == 0 else (),
                                inc="tT" if ic == 7 else None)
                    last_mm = tok_mms[1]
                    for tg in range(2):
                        ev = W.do("vector", lambda e, cc=cc, tg=tg,
                                  ps=ps2[tg]: e.scalar_tensor_tensor(
                            x2_sb[:, cc * 1024 + tg * 512:
                                  cc * 1024 + (tg + 1) * 512],
                            ps, H["dsc_sb"][:, cc:cc + 1],
                            x2_sb[:, cc * 1024 + tg * 512:
                                  cc * 1024 + (tg + 1) * 512],
                            op0=ALU.mult, op1=ALU.add),
                            deps=[tok_mms[tg], war_x2], inc="vV")
                        war_slot[(cc % 2) * 2 + tg] = [ev]
                        tok_out_ready = ev
                war_wbuf[wpar] = [last_mm]
                war_hact = [last_mm]

    tok_out = W.do("sync", lambda e: e.dma_start(
        out=H["out"].rearrange("(c p2) t -> p2 c t", p2=128),
        in_=x2_sb[:, :].rearrange("p (c t) -> p c t", t=1024)),
        deps=[tok_out_ready, ("vV", W.cnt["vV"])], inc="qS", n=16)
    W.wait_only("sync", [tok_out])
    W.wait_only("gpsimd", [tok_out])



_BF = ml_dtypes.bfloat16


def _masks_np():
    m = np.zeros((128, 2048), dtype=np.float32)
    q = np.arange(1024)
    for rel in range(2):
        for half in range(2):
            r = 2 * rel + half
            k = np.arange(128)[:, None]
            qq = q[None, half * 512:(half + 1) * 512] - half * 512
            m[:, rel * 1024 + half * 512: rel * 1024 + (half + 1) * 512] = (
                (128 * r + k) <= qq).astype(np.float32)
    return m.astype(_BF)


def kernel(L0_attn_x, L0_attn_w_attn, L0_attn_b_attn, L0_attn_w_proj,
           L0_attn_b_proj, L0_ffn_gate_w, L0_ffn_gate_scale, L0_ffn_up_w,
           L0_ffn_up_scale, L0_ffn_down_w, L0_ffn_down_scale):
    x = np.asarray(L0_attn_x, dtype=np.float32)[0]
    w_attn = np.asarray(L0_attn_w_attn, dtype=np.float32)
    b_attn = np.asarray(L0_attn_b_attn, dtype=np.float32)
    w_proj = np.asarray(L0_attn_w_proj, dtype=np.float32)
    b_proj = np.asarray(L0_attn_b_proj, dtype=np.float32)
    gate_w = np.asarray(L0_ffn_gate_w, dtype=np.int32)
    up_w = np.asarray(L0_ffn_up_w, dtype=np.int32)
    down_w = np.asarray(L0_ffn_down_w, dtype=np.int32)
    gate_s = np.asarray(L0_ffn_gate_scale, dtype=np.float32)
    up_s = np.asarray(L0_ffn_up_scale, dtype=np.float32)
    down_s = np.asarray(L0_ffn_down_scale, dtype=np.float32)

    xT = np.ascontiguousarray(x.T)
    gateT = np.ascontiguousarray(gate_w.T)
    upT = np.ascontiguousarray(up_w.T)
    downT = np.ascontiguousarray(down_w.T)
    gsc = np.ascontiguousarray(gate_s.reshape(64, 128).T)
    usc = np.ascontiguousarray(up_s.reshape(64, 128).T)
    dsc = np.ascontiguousarray(down_s.reshape(16, 128).T)
    bproj = np.ascontiguousarray(b_proj.reshape(16, 128).T)
    masks = _masks_np()
    ones_c = np.ones((128, 1), dtype=_BF)
    ones_r = np.ones((1, 128), dtype=np.float32)

    in_maps = []
    for k in range(NC):
        cols = np.concatenate([
            w_attn[:, 256 * k:256 * (k + 1)],
            w_attn[:, 2048 + 256 * k:2048 + 256 * (k + 1)],
            w_attn[:, 4096 + 256 * k:4096 + 256 * (k + 1)]], axis=1)
        bq = b_attn[256 * k:256 * (k + 1)]
        bk = b_attn[2048 + 256 * k:2048 + 256 * (k + 1)]
        bv = b_attn[4096 + 256 * k:4096 + 256 * (k + 1)]
        bcol = np.stack([bq[:128], bq[128:], bk[:128], bk[128:]], axis=1)
        own_rows = np.concatenate(
            [x[1024 * j + 128 * k:1024 * j + 128 * (k + 1)] for j in range(8)])
        in_maps.append(dict(
            xfeat=xT,
            xown=np.ascontiguousarray(own_rows.T),
            watt=np.ascontiguousarray(cols),
            bcol=np.ascontiguousarray(bcol),
            bvrow=np.ascontiguousarray(bv.reshape(1, 256)),
            wpj=np.ascontiguousarray(w_proj[256 * k:256 * (k + 1), :]),
            bproj=bproj if k == 0 else np.zeros_like(bproj),
            gateT=gateT, upT=upT, downT=downT,
            gsc=gsc, usc=usc, dsc=dsc,
            maskp=masks, onescol=ones_c, onesrow=ones_r,
        ))

    nc = build_program()
    res = run_bass_kernel_spmd(nc, in_maps, core_ids=list(range(NC)))

    full = np.empty((T, C), dtype=np.float32)
    for k in range(NC):
        o = res.results[k]["out"]
        for j in range(8):
            full[1024 * j + 128 * k:1024 * j + 128 * (k + 1), :] = \
                o[:, 128 * j:128 * (j + 1)].T
    return full.reshape(1, T, C)
